# version 24
# baseline (speedup 1.0000x reference)
import numpy as np
import concourse.bass as bass
import concourse.mybir as mybir
from concourse.bass_utils import run_bass_kernel_spmd

F32 = mybir.dt.float32
BF16 = mybir.dt.bfloat16
AF = mybir.ActivationFunctionType
ALU = mybir.AluOpType
AX = mybir.AxisListType

B = 2
S = 2048
D = 1024
I = 4096
TP = 128
NTB = S // TP
NT = B * NTB
SCALE = float(D) ** -0.5
EPS = 1e-5
KB = 1024

ENG_SEM = {"tensor": "T", "vector": "V", "scalar": "S", "gpsimd": "G"}
DMA_SEM = {"sync": "DS", "scalar": "DC", "gpsimd": "DG"}
NLANES = 6
DMA_LANES = {q: ["%s%d" % (q, i) for i in range(NLANES)] for q in ["DS", "DC", "DG"]}
ALL_SEMS = (["T", "V", "S", "G"]
            + DMA_LANES["DS"] + DMA_LANES["DC"] + DMA_LANES["DG"])


class Buf:
    __slots__ = ("writer", "readers", "aliases", "name")

    def __init__(self, name=""):
        self.writer = None
        self.readers = {}
        self.aliases = []
        self.name = name


class Prog:
    def __init__(self):
        self.counts = {s: 0 for s in ALL_SEMS}
        self.items = {e: [] for e in ["tensor", "vector", "scalar", "gpsimd", "sync"]}
        self.lastw = {}
        self.rr = {q: 0 for q in ["DS", "DC", "DG"]}

    def op(self, eng, build, reads=(), writes=(), dma=False):
        if dma:
            q = DMA_SEM[eng]
            sem = DMA_LANES[q][self.rr[q] % NLANES]
            self.rr[q] += 1
        else:
            sem = ENG_SEM[eng]
        amt = 16 if dma else 1
        waits = {}

        def need(mark):
            if not mark:
                return
            s, v = mark
            if s == "T" and eng == "tensor" and not dma:
                return
            if self.lastw.get((eng, s), 0) >= v:
                return
            if waits.get(s, 0) < v:
                waits[s] = v

        def need_all(b):
            need(b.writer)
            for s, v in b.readers.items():
                need((s, v))

        if dma and self.counts[sem] > 0:
            need((sem, self.counts[sem]))
        for b in reads:
            need(b.writer)
        for b in writes:
            need_all(b)
            for ab in b.aliases:
                need_all(ab)
        self.counts[sem] += amt
        mark = (sem, self.counts[sem])
        for s, v in waits.items():
            if self.lastw.get((eng, s), 0) < v:
                self.lastw[(eng, s)] = v
        self.items[eng].append((waits, build, sem, amt))
        for b in reads:
            if b.readers.get(sem, 0) < mark[1]:
                b.readers[sem] = mark[1]
        for b in writes:
            b.writer = mark
            b.readers = {}
        return mark


def _bpp(shape, dtype):
    n = 1
    for d in shape[1:]:
        n *= d
    return n * mybir.dt.size(dtype)


def build_nc():
    nc = bass.Bass("TRN2", target_bir_lowering=False)
    pr = Prog()
    o = pr.op

    x_h = nc.dram_tensor("x", [NT * TP, D], F32, kind="ExternalInput").ap()
    wq_h = nc.dram_tensor("wq", [D, D], F32, kind="ExternalInput").ap()
    wk_h = nc.dram_tensor("wk", [D, D], F32, kind="ExternalInput").ap()
    wv_h = nc.dram_tensor("wv", [D, D], F32, kind="ExternalInput").ap()
    wr_h = nc.dram_tensor("wr", [D, D], F32, kind="ExternalInput").ap()
    wgT_h = nc.dram_tensor("wgT", [D, I], F32, kind="ExternalInput").ap()
    wuT_h = nc.dram_tensor("wuT", [D, I], F32, kind="ExternalInput").ap()
    wdT_h = nc.dram_tensor("wdT", [I, D], F32, kind="ExternalInput").ap()
    y_h = nc.dram_tensor("y", [NT * TP, D], F32, kind="ExternalOutput").ap()
    sacc_h = nc.dram_tensor("sacc", [NT * TP, D], F32, kind="Internal").ap()
    sxi_h = nc.dram_tensor("sxi", [NT * TP, D], BF16, kind="Internal").ap()
    sV_h = nc.dram_tensor("sV", [NT * TP, D], BF16, kind="Internal").ap()
    shqT_h = nc.dram_tensor("shqT", [NT * TP, I], BF16, kind="Internal").ap()
    swdT_h = nc.dram_tensor("swdT", [I, D], BF16, kind="Internal").ap()

    sacc_b = [Buf("sacc%d" % t) for t in range(NT)]
    sxi_b = [Buf("sxi%d" % t) for t in range(NT)]
    sV_b = [Buf("sV%d" % t) for t in range(NT)]
    shqT_b = [Buf("shqT%d" % t) for t in range(NT)]
    swdT_b = [Buf("swdT%d" % k) for k in range(32)]

    allocs = []

    def sb(name, shape, dtype, off_kb, nbuf=1):
        off = int(off_kb * KB)
        h = nc.alloc_sbuf_tensor_at(name, shape, dtype, offset=off)
        end = off + _bpp(shape, dtype)
        assert end <= int(207.8 * KB), (name, end)
        bufs = Buf(name) if nbuf == 1 else [Buf("%s%d" % (name, i)) for i in range(nbuf)]
        bl = bufs if isinstance(bufs, list) else [bufs]
        for (s0, e0, bl0) in allocs:
            if s0 < end and off < e0:
                for bb in bl:
                    for ob in bl0:
                        bb.aliases.append(ob)
                        ob.aliases.append(bb)
        allocs.append((off, end, bl))
        return h.ap(), bufs

    ident, ident_b = sb("ident", [128, 128], BF16, 18.0)
    ones_f, ones_b = sb("ones_f", [128, 128], F32, 18.25)
    svec, svec_b = sb("svec", [128, NT], F32, 18.75)
    xinvv, xinvv_b = sb("xinvv", [128, NT], F32, 18.875)
    hinvv, hinvv_b = sb("hinvv", [128, NT], F32, 19.0)
    cgv, cgv_b = sb("cgv", [128, NT], F32, 19.125)
    cuv, cuv_b = sb("cuv", [128, NT], F32, 19.25)
    cdv, cdv_b = sb("cdv", [128, NT], F32, 19.375)
    hsv, hsv_b = sb("hsv", [128, NT], F32, 19.5)
    epsv, epsv_b = sb("epsv", [128, 1], F32, 19.625)
    cbias, cbias_b = sb("cbias", [128, 8], F32, 19.75)
    halfv, halfv_b = sb("halfv", [128, 1], BF16, 19.8125)
    bm5, bm5_b = sb("bm5", [128, 1], F32, 19.875)
    bp5, bp5_b = sb("bp5", [128, 1], F32, 19.9375)
    wsum, wsum_b = sb("wsum", [128, 96], F32, 20.0)
    wstat, wstat_b = sb("wstat", [128, 16], F32, 20.375)
    stat, stat_bl = sb("stat", [128, 32], F32, 20.4375, nbuf=4)
    gk_s, gk_s_b = sb("gk_s", [128, 2 * D], F32, 20.625)
    gqbc, gqbc_b = sb("gqbc", [128, D], F32, 28.625)
    gkbc, gkbc_b = sb("gkbc", [128, D], F32, 32.625)
    iotat, iotat_b = sb("iotat", [128, 128], F32, 37.125)
    Kst, Kst_bl = sb("Kst", [128, NTB * D], BF16, 37.125, nbuf=NTB)
    wq_sb, wq_sb_b = sb("wq_sb", [128, 8 * D], BF16, 69.125)
    wk_sb, wk_sb_b = sb("wk_sb", [128, 8 * D], BF16, 85.125)
    wv_sb, wv_sb_b = sb("wv_sb", [128, 8 * D], BF16, 101.125)
    NXB = 6
    xbuf_t, xbuf_bl4 = sb("xbuf", [128, 4 * D], F32, 117.125, nbuf=4)
    xbuf2_t, xbuf_bl2 = sb("xbuf2", [128, 2 * D], F32, 171.375, nbuf=2)
    xbuf = ([xbuf_t[:, i * D:(i + 1) * D] for i in range(4)]
            + [xbuf2_t[:, i * D:(i + 1) * D] for i in range(2)])
    xbuf_bl = xbuf_bl4 + xbuf_bl2
    xnf_t, xnf_bl = sb("xnf", [128, D], F32, 133.125, nbuf=1)
    xnf2_t, xnf2_bl = sb("xnf2", [128, D], F32, 179.375, nbuf=1)
    xnf_s = [xnf_t, xnf2_t]
    xnf_sbl = [xnf_bl, xnf2_bl]
    sqs, sqs_b = sb("sqs", [128, D], F32, 137.125)
    xi16b_t, xi16b_bl = sb("xi16b", [128, D], BF16, 183.375)
    accs2_t, accs2_bl = sb("accs2", [128, D], F32, 185.375)
    v16b_t, v16b_bl = sb("v16b", [128, D], BF16, 189.375)
    e_sb, e_sb_b = sb("e_sb", [128, D], F32, 141.125)
    accs, accs_b = sb("accs", [128, D], F32, 145.125)
    lh1, lh1_b = sb("lh1", [128, D], F32, 149.125)
    lh2, lh2_b = sb("lh2", [128, D], F32, 153.125)
    xn16_t, xn16_bl = sb("xn16", [128, 2 * D], BF16, 157.125, nbuf=2)
    xn16 = [xn16_t[:, i * D:(i + 1) * D] for i in range(2)]
    xnT_t, xnT_bl = sb("xnT_s", [128, 2 * D], BF16, 161.125, nbuf=2)
    xnT_s = [xnT_t[:, i * D:(i + 1) * D] for i in range(2)]
    xi16, xi16_b = sb("xi16", [128, D], BF16, 165.125)
    v16, v16_b = sb("v16", [128, D], BF16, 167.125)
    w2_s, w2_s_b = sb("w2_s", [128, D], BF16, 169.125)
    rep_s, rep_s_b = sb("rep_s", [128, 128], BF16, 171.125)
    wf32_t, wf32_bl = sb("wf32", [128, 2 * D], F32, 197.375, nbuf=2)
    wf32 = [wf32_t[:, i * D:(i + 1) * D] for i in range(2)]
    wabs, wabs_b = sb("wabs", [128, D], BF16, 205.375)
    p2_pbuf, p2_pbuf_b = sb("p2_pbuf", [128, D], F32, 117.15625)
    p2_e, p2_e_b = sb("p2_e", [128, D], F32, 121.15625)
    p2_w2, p2_w2_b = sb("p2_w2", [128, D], BF16, 125.15625)
    p2_rep, p2_rep_b = sb("p2_rep", [128, 128], BF16, 127.15625)
    tstA, tstA_b = sb("tstA", [128, D], BF16, 205.375)
    wdst_t, wdst_bl = sb("wdst", [128, 2 * D], BF16, 69.125, nbuf=2)
    wdst = [wdst_t[:, i * D:(i + 1) * D] for i in range(2)]
    wf32x_t, wf32x_bl = sb("wf32x", [128, 3 * D], F32, 37.125, nbuf=3)
    wgT_sb, wgT_sb_b = sb("wgT_sb", [128, 8 * I], BF16, 69.125)
    wuT_sb, wuT_sb_b = sb("wuT_sb", [128, 8 * I], BF16, 133.125)
    f_xi_t, f_xi_bl = sb("f_xi", [128, 2 * D], BF16, 37.125, nbuf=2)
    f_xi = [f_xi_t[:, i * D:(i + 1) * D] for i in range(2)]
    f_xiT_t, f_xiT_bl = sb("f_xiT", [128, 2 * D], BF16, 41.125, nbuf=2)
    f_xiT = [f_xiT_t[:, i * D:(i + 1) * D] for i in range(2)]
    f_g_t, f_g_bl = sb("f_g", [128, 2 * 512], F32, 45.125, nbuf=2)
    f_g = [f_g_t[:, i * 512:(i + 1) * 512] for i in range(2)]
    f_h, f_h_b = sb("f_h", [128, I], F32, 49.125)
    fl1, fl1_b = sb("fl1", [128, D], F32, 28.625)
    fl2, fl2_b = sb("fl2", [128, D], F32, 32.625)
    rA, rA_b = sb("rA", [128, D], BF16, 65.125)
    rB, rB_b = sb("rB", [128, D], BF16, 67.125)
    f_hq, f_hq_b = sb("f_hq", [128, I], BF16, 197.375)
    wr_sb, wr_sb_b = sb("wr_sb", [128, 8 * D], BF16, 37.125)
    wdT_sb, wdT_sb_b = sb("wdT_sb", [128, 32 * D], BF16, 133.125)
    z_v_t, z_v_bl = sb("z_v", [128, 2 * D], BF16, 53.125, nbuf=2)
    z_v = [z_v_t[:, i * D:(i + 1) * D] for i in range(2)]
    z_u, z_u_b = sb("z_u", [128, D], BF16, 57.125)
    z_uT_t, z_uT_bl = sb("z_uT", [128, 2 * D], BF16, 59.125, nbuf=2)
    z_uT = [z_uT_t[:, i * D:(i + 1) * D] for i in range(2)]
    z_a2, z_a2_b = sb("z_a2", [128, D], F32, 63.125)
    nrs_sb, nrs_sb_b = sb("nrs_sb", [1, D], F32, 125.125)
    z_hq_t, z_hq_bl = sb("z_hq", [128, 2 * I], BF16, 69.125, nbuf=2)
    z_hq = [z_hq_t[:, i * I:(i + 1) * I] for i in range(2)]
    z_hqT_t, z_hqT_bl = sb("z_hqT", [128, 2 * I], BF16, 85.125, nbuf=2)
    z_hqT = [z_hqT_t[:, i * I:(i + 1) * I] for i in range(2)]
    z_acc_t, z_acc_bl = sb("z_acc", [128, 4 * D], F32, 101.125, nbuf=4)
    z_acc = [z_acc_t[:, i * D:(i + 1) * D] for i in range(4)]
    z_y_t, z_y_bl = sb("z_y", [128, 2 * D], F32, 117.125, nbuf=2)
    z_y = [z_y_t[:, i * D:(i + 1) * D] for i in range(2)]
    corr_bc, corr_bc_b = sb("corr_bc", [128, D], F32, 28.625)

    PA_h = nc.alloc_psum_tensor("PA", [128, D], F32)
    PB_h = nc.alloc_psum_tensor("PB", [128, D], F32)
    PC_h = nc.alloc_psum_tensor("PC", [128, D], F32)
    PT_h = nc.alloc_psum_tensor("PT", [128, D], BF16)
    P7_h = nc.alloc_psum_tensor("P7", [128, 512], F32)
    PA, PB, PC, PT, P7 = PA_h.ap(), PB_h.ap(), PC_h.ap(), PT_h.ap(), P7_h.ap()
    PTf = PT.bitcast(F32)
    PA_bl = [Buf("PA0"), Buf("PA1")]
    PB_bl = [Buf("PB0"), Buf("PB1")]
    PC_bl = [Buf("PC0"), Buf("PC1")]
    PT_b = Buf("PT")
    P7_b = Buf("P7")

    o("gpsimd", lambda g: g.iota(iotat[:], [[1, 128]], channel_multiplier=-1,
                                 allow_small_or_imprecise_dtypes=True),
      writes=[iotat_b])
    o("vector", lambda v: v.tensor_single_scalar(ident[:], iotat[:], 0.0, ALU.is_equal),
      reads=[iotat_b], writes=[ident_b])
    o("vector", lambda v: v.memset(ones_f[:], 1.0), writes=[ones_b])
    o("vector", lambda v: v.memset(gqbc[:], 0.0), writes=[gqbc_b])
    o("vector", lambda v: v.memset(gkbc[:], 0.0), writes=[gkbc_b])
    o("vector", lambda v: v.memset(wsum[:], 0.0), writes=[wsum_b])
    o("vector", lambda v: v.memset(wstat[:], 0.0), writes=[wstat_b])
    o("vector", lambda v: v.memset(epsv[:], EPS), writes=[epsv_b])
    o("vector", lambda v: v.memset(halfv[:], 0.5), writes=[halfv_b])
    o("vector", lambda v: v.memset(bm5[:], -0.5), writes=[bm5_b])
    o("vector", lambda v: v.memset(bp5[:], 0.5), writes=[bp5_b])
    for j in range(8):
        cval = -float(2 ** (7 - j)) * (2.0 ** 40)
        o("vector", (lambda j=j, cval=cval: lambda v: v.memset(
            cbias[:, j:j + 1], cval))(), writes=[cbias_b])

    for w_hh, w_sbb, w_bb in [(wq_h, wq_sb, wq_sb_b), (wk_h, wk_sb, wk_sb_b),
                              (wv_h, wv_sb, wv_sb_b)]:
        for k in range(8):
            o("gpsimd",
              (lambda w_hh=w_hh, w_sbb=w_sbb, k=k:
               lambda g: g.dma_start(out=w_sbb[:, k * D:(k + 1) * D],
                                     in_=w_hh[k * 128:(k + 1) * 128, :]))(),
              writes=[w_bb], dma=True)

    passA = ([("wg", k) for k in range(32)] + [("wu", k) for k in range(32)]
             + [("wd", k) for k in range(32)])

    def emit_passA(idx):
        nm, k = passA[idx]
        slot = idx % 2
        if nm == "wd":
            src = wdT_h[k * 128:(k + 1) * 128, :]
        else:
            w_hh = wgT_h if nm == "wg" else wuT_h
            src = w_hh[(k // 4) * 128:(k // 4 + 1) * 128, (k % 4) * D:(k % 4 + 1) * D]
        col = {"wg": 0, "wu": 32, "wd": 64}[nm] + k
        o("scalar", (lambda src=src, slot=slot:
                     lambda s: s.dma_start(out=wf32[slot], in_=src))(),
          writes=[wf32_bl[slot]], dma=True)
        o("scalar", (lambda slot=slot, col=col:
                     lambda s: s.activation(wabs[:], wf32[slot], AF.Abs,
                                            accum_out=wsum[:, col:col + 1]))(),
          reads=[wf32_bl[slot]], writes=[wabs_b, wsum_b])

    pa_i = [0]

    def drip_passA(n):
        for _ in range(n):
            if pa_i[0] < len(passA):
                emit_passA(pa_i[0])
                pa_i[0] += 1

    def stA(t):
        slot = t % NXB
        o("sync", (lambda t=t, slot=slot:
                   lambda s: s.dma_start(out=xbuf[slot], in_=x_h[t * TP:(t + 1) * TP, :]))(),
          writes=[xbuf_bl[slot]], dma=True)

    def stB(t):
        slot = t % 4
        ss = slot * 8
        sb_ = stat_bl[slot]
        xb = xbuf[t % NXB]
        xbb = xbuf_bl[t % NXB]
        c = lambda i: stat[:, ss + i:ss + i + 1]
        o("vector", (lambda xb=xb, ss=ss: lambda v: v.reduce_sum(
            stat[:, ss:ss + 1], xb, axis=AX.X))(), reads=[xbb], writes=[sb_])
        o("vector", (lambda ss=ss: lambda v: v.tensor_scalar_mul(
            stat[:, ss + 1:ss + 2], stat[:, ss:ss + 1], 1.0 / D))(),
          reads=[sb_], writes=[sb_])
        o("scalar", (lambda xb=xb, ss=ss: lambda s: s.activation(
            sqs[:], xb, AF.Square, accum_out=stat[:, ss + 2:ss + 3]))(),
          reads=[xbb], writes=[sqs_b, sb_])
        o("vector", (lambda ss=ss: lambda v: v.tensor_scalar_mul(
            stat[:, ss + 2:ss + 3], stat[:, ss + 2:ss + 3], 1.0 / D))(),
          reads=[sb_], writes=[sb_])
        o("vector", (lambda ss=ss: lambda v: v.scalar_tensor_tensor(
            stat[:, ss + 3:ss + 4], stat[:, ss + 1:ss + 2], stat[:, ss + 1:ss + 2],
            stat[:, ss + 2:ss + 3], ALU.mult, ALU.subtract))(),
          reads=[sb_], writes=[sb_])
        o("scalar", (lambda ss=ss: lambda s: s.activation(
            stat[:, ss + 4:ss + 5], stat[:, ss + 3:ss + 4], AF.Ln,
            scale=-1.0, bias=epsv[:, 0:1]))(), reads=[sb_, epsv_b], writes=[sb_])
        o("scalar", (lambda ss=ss: lambda s: s.activation(
            stat[:, ss + 5:ss + 6], stat[:, ss + 4:ss + 5], AF.Exp,
            scale=-0.5))(), reads=[sb_], writes=[sb_])
        xnf = xnf_s[t % 2]
        xnf_b = xnf_sbl[t % 2]
        xi16_, xi16_b_ = (xi16, xi16_b) if t % 2 == 0 else (xi16b_t, xi16b_bl)
        o("vector", (lambda xb=xb, ss=ss, xnf=xnf: lambda v: v.tensor_scalar(
            xnf[:], xb, stat[:, ss + 1:ss + 2], stat[:, ss + 5:ss + 6],
            ALU.subtract, ALU.mult))(),
          reads=[xbb, sb_], writes=[xnf_b])
        o("vector", (lambda ss=ss, xnf=xnf: lambda v: v.reduce_max(
            stat[:, ss + 7:ss + 8], xnf[:], axis=AX.X, apply_absolute_value=True))(),
          reads=[xnf_b], writes=[sb_])
        o("vector", (lambda ss=ss: lambda v: v.tensor_scalar_max(
            stat[:, ss + 7:ss + 8], stat[:, ss + 7:ss + 8], 1e-5))(),
          reads=[sb_], writes=[sb_])
        o("vector", (lambda t=t, ss=ss: lambda v: v.tensor_scalar_mul(
            xinvv[:, t:t + 1], stat[:, ss + 7:ss + 8], 1.0 / 127.0))(),
          reads=[sb_], writes=[xinvv_b])
        o("vector", (lambda ss=ss: lambda v: v.reciprocal(
            stat[:, ss + 7:ss + 8], stat[:, ss + 7:ss + 8]))(),
          reads=[sb_], writes=[sb_])
        o("vector", (lambda t=t, ss=ss: lambda v: v.tensor_scalar_mul(
            svec[:, t:t + 1], stat[:, ss + 7:ss + 8], 127.0))(),
          reads=[sb_], writes=[svec_b])
        o("scalar", (lambda t=t, xnf=xnf: lambda s: s.copy(xn16[t % 2], xnf[:]))(),
          reads=[xnf_b], writes=[xn16_bl[t % 2]])
        o("vector", (lambda t=t, xnf=xnf: lambda v: v.tensor_scalar(
            lh1[:], xnf[:], svec[:, t:t + 1], 128.5, ALU.mult, ALU.add))(),
          reads=[xnf_b, svec_b], writes=[lh1_b])
        for kb in range(7, -1, -1):
            cc = float(2 ** kb)
            o("vector", (lambda cc=cc: lambda v: v.tensor_scalar(
                lh2[:], lh1[:], cc, cc, ALU.is_ge, ALU.mult))(),
              reads=[lh1_b], writes=[lh2_b])
            o("vector", lambda v: v.tensor_tensor(lh1[:], lh1[:], lh2[:], ALU.subtract),
              reads=[lh2_b], writes=[lh1_b])
        o("vector", (lambda t=t, xnf=xnf: lambda v: v.tensor_scalar(
            lh2[:], xnf[:], svec[:, t:t + 1], 0.5, ALU.mult, ALU.add))(),
          reads=[xnf_b, svec_b], writes=[lh2_b])
        o("vector", (lambda xi16_=xi16_: lambda v: v.tensor_tensor(
            xi16_[:], lh2[:], lh1[:], ALU.subtract))(),
          reads=[lh2_b, lh1_b], writes=[xi16_b_])
        o("gpsimd", (lambda t=t, xi16_=xi16_: lambda g: g.dma_start(
            out=sxi_h[t * TP:(t + 1) * TP, :], in_=xi16_[:]))(),
          reads=[xi16_b_], writes=[sxi_b[t]], dma=True)

    def stC(t):
        s2 = t % 2

        def tr(te):
            last = None
            for k in range(8):
                last = te.transpose(PT[:, k * 128:(k + 1) * 128],
                                    xn16[s2][:, k * 128:(k + 1) * 128], ident[:])
            return last
        o("tensor", tr, reads=[xn16_bl[s2], ident_b], writes=[PT_b])
        o("scalar", (lambda s2=s2: lambda s: s.copy(xnT_s[s2], PT[:]))(),
          reads=[PT_b], writes=[xnT_bl[s2]])

        def qkv(te):
            last = None
            for k in range(8):
                lhsT = xnT_s[s2][:, k * 128:(k + 1) * 128]
                for (ws, ps) in [(wq_sb, PA), (wk_sb, PB), (wv_sb, PC)]:
                    for h in range(2):
                        last = te.matmul(ps[:, h * 512:(h + 1) * 512], lhsT,
                                         ws[:, k * D + h * 512:k * D + (h + 1) * 512],
                                         start=(k == 0), stop=(k == 7))
            return last
        o("tensor", qkv, reads=[xnT_bl[s2], wq_sb_b, wk_sb_b, wv_sb_b],
          writes=PA_bl + PB_bl + PC_bl)

    def stD(t):
        slot = t % 4
        ss = slot * 8
        sb_ = stat_bl[slot]
        tb = t % NTB
        o("scalar", (lambda tb=tb: lambda s: s.copy(
            Kst[:, tb * D:(tb + 1) * D], PB[:]))(),
          reads=PB_bl, writes=[Kst_bl[tb]])
        v16_, v16_b_ = (v16, v16_b) if t % 2 == 0 else (v16b_t, v16b_bl)
        o("scalar", (lambda v16_=v16_: lambda s: s.copy(v16_[:], PC[:]))(),
          reads=PC_bl, writes=[v16_b_])
        o("scalar", (lambda t=t, v16_=v16_: lambda s: s.dma_start(
            out=sV_h[t * TP:(t + 1) * TP, :], in_=v16_[:]))(),
          reads=[v16_b_], writes=[sV_b[t]], dma=True)
        o("vector", (lambda ss=ss: lambda v: v.reduce_max(
            stat[:, ss + 7:ss + 8], PA[:], axis=AX.X))(),
          reads=PA_bl, writes=[sb_])
        o("scalar", (lambda ss=ss: lambda s: s.mul(
            stat[:, ss + 6:ss + 7], stat[:, ss + 7:ss + 8], -SCALE))(),
          reads=[sb_], writes=[sb_])
        o("scalar", (lambda ss=ss: lambda s: s.activation(
            e_sb[:], PA[:], AF.Exp, bias=stat[:, ss + 6:ss + 7], scale=SCALE,
            accum_out=stat[:, ss + 5:ss + 6]))(),
          reads=PA_bl + [sb_], writes=[e_sb_b, sb_])
        o("vector", (lambda ss=ss: lambda v: v.reciprocal(
            stat[:, ss + 4:ss + 5], stat[:, ss + 5:ss + 6]))(),
          reads=[sb_], writes=[sb_])
        o("vector", (lambda ss=ss: lambda v: v.tensor_scalar(
            rep_s[:], ones_f[:], stat[:, ss + 4:ss + 5], None, ALU.mult))(),
          reads=[ones_b, sb_], writes=[rep_s_b])
        o("vector", lambda v: v.tensor_tensor(w2_s[:], PA[:], e_sb[:], ALU.mult),
          reads=PA_bl + [e_sb_b], writes=[w2_s_b])
        accs_, accs_b_ = (accs, accs_b) if t % 2 == 0 else (accs2_t, accs2_bl)
        xslot = t % NXB
        o("vector", (lambda xslot=xslot, accs_=accs_: lambda v: v.tensor_tensor(
            accs_[:], xbuf[xslot], PA[:], ALU.add))(),
          reads=[xbuf_bl[xslot]] + PA_bl, writes=[accs_b_])
        o("sync", (lambda t=t, accs_=accs_: lambda s: s.dma_start(
            out=sacc_h[t * TP:(t + 1) * TP, 0:512], in_=accs_[:, 0:512]))(),
          reads=[accs_b_], writes=[sacc_b[t]], dma=True)
        o("scalar", (lambda t=t, accs_=accs_: lambda s: s.dma_start(
            out=sacc_h[t * TP:(t + 1) * TP, 512:D], in_=accs_[:, 512:D]))(),
          reads=[accs_b_], writes=[sacc_b[t]], dma=True)
        for h in range(2):
            o("tensor", (lambda h=h: lambda te: te.matmul(
                P7[:], rep_s[:], w2_s[:, h * 512:(h + 1) * 512],
                start=True, stop=True))(),
              reads=[rep_s_b, w2_s_b], writes=[P7_b])
            o("vector", (lambda h=h: lambda v: v.tensor_tensor(
                gqbc[:, h * 512:(h + 1) * 512], gqbc[:, h * 512:(h + 1) * 512],
                P7[:], ALU.add))(),
              reads=[P7_b], writes=[gqbc_b])

    def p2_tile(t):
        slot = t % 4
        ss = slot * 8
        sb_ = stat_bl[slot]
        tb = t % NTB
        o("vector", (lambda tb=tb: lambda v: v.tensor_tensor(
            p2_pbuf[:], gqbc[:], Kst[:, tb * D:(tb + 1) * D], ALU.mult))(),
          reads=[gqbc_b, Kst_bl[tb]], writes=[p2_pbuf_b])
        o("vector", (lambda ss=ss: lambda v: v.reduce_max(
            stat[:, ss + 7:ss + 8], p2_pbuf[:], axis=AX.X))(),
          reads=[p2_pbuf_b], writes=[sb_])
        o("scalar", (lambda ss=ss: lambda s: s.mul(
            stat[:, ss + 6:ss + 7], stat[:, ss + 7:ss + 8], -SCALE))(),
          reads=[sb_], writes=[sb_])
        o("scalar", (lambda ss=ss: lambda s: s.activation(
            p2_e[:], p2_pbuf[:], AF.Exp, bias=stat[:, ss + 6:ss + 7], scale=SCALE,
            accum_out=stat[:, ss + 5:ss + 6]))(),
          reads=[p2_pbuf_b, sb_], writes=[p2_e_b, sb_])
        o("vector", (lambda ss=ss: lambda v: v.reciprocal(
            stat[:, ss + 4:ss + 5], stat[:, ss + 5:ss + 6]))(),
          reads=[sb_], writes=[sb_])
        o("vector", (lambda ss=ss: lambda v: v.tensor_scalar(
            p2_rep[:], ones_f[:], stat[:, ss + 4:ss + 5], None, ALU.mult))(),
          reads=[ones_b, sb_], writes=[p2_rep_b])
        o("vector", lambda v: v.tensor_tensor(p2_w2[:], p2_pbuf[:], p2_e[:], ALU.mult),
          reads=[p2_pbuf_b, p2_e_b], writes=[p2_w2_b])
        for h in range(2):
            o("tensor", (lambda h=h: lambda te: te.matmul(
                P7[:], p2_rep[:], p2_w2[:, h * 512:(h + 1) * 512],
                start=True, stop=True))(),
              reads=[p2_rep_b, p2_w2_b], writes=[P7_b])
            o("vector", (lambda h=h: lambda v: v.tensor_tensor(
                gkbc[:, h * 512:(h + 1) * 512], gkbc[:, h * 512:(h + 1) * 512],
                P7[:], ALU.add))(),
              reads=[P7_b], writes=[gkbc_b])

    def p1_batch(bi, rate, drip=None):
        base = bi * NTB
        stA(base)
        stA(base + 1)
        for s_ in range(NTB + 2):
            if s_ + 2 < NTB:
                stA(base + s_ + 2)
            if 2 <= s_ <= NTB + 1:
                stD(base + s_ - 2)
            if 1 <= s_ <= NTB:
                stC(base + s_ - 1)
            if s_ < NTB:
                stB(base + s_)
            drip_passA(rate)
            if drip is not None:
                drip()

    passB = ([("wd", k) for k in range(32)] + [("wu", k) for k in range(32)]
             + [("wg", k) for k in range(24)] + [("wg", k) for k in range(24, 32)])

    wf32a = wf32 + [wf32x_t[:, i * D:(i + 1) * D] for i in range(3)]
    wf32a_bl = list(wf32_bl) + list(wf32x_bl)

    def emit_passB(idx):
        nm, k = passB[idx]
        slot = idx % 2
        lslot = idx % 2 if idx < 40 else 2 + (idx % 3)
        if nm == "wd":
            src_ = wdT_h[k * 128:(k + 1) * 128, :]
            ridx = 10
            oap, tgt_b = wdst[slot], wdst_bl[slot]
        else:
            w_hh = wgT_h if nm == "wg" else wuT_h
            src_ = w_hh[(k // 4) * 128:(k // 4 + 1) * 128, (k % 4) * D:(k % 4 + 1) * D]
            ridx = 8 if nm == "wg" else 9
            tgt = wgT_sb if nm == "wg" else wuT_sb
            tgt_b = wgT_sb_b if nm == "wg" else wuT_sb_b
            oap = tgt[:, (k // 4) * I + (k % 4) * D:(k // 4) * I + (k % 4 + 1) * D]
        o(["sync", "scalar", "gpsimd"][idx % 3],
          (lambda src_=src_, lslot=lslot:
           lambda s: s.dma_start(out=wf32a[lslot], in_=src_))(),
          writes=[wf32a_bl[lslot]], dma=True)
        o("scalar", (lambda lslot=lslot, ridx=ridx: lambda s: s.activation(
            tstA[:], wf32a[lslot], AF.Sign, scale=wstat[:, ridx:ridx + 1],
            bias=bm5[:, 0:1]))(),
          reads=[wf32a_bl[lslot], wstat_b, bm5_b], writes=[tstA_b])
        o("scalar", (lambda lslot=lslot, ridx=ridx, oap=oap: lambda s: s.activation(
            oap, wf32a[lslot], AF.Sign, scale=wstat[:, ridx:ridx + 1],
            bias=bp5[:, 0:1]))(),
          reads=[wf32a_bl[lslot], wstat_b, bp5_b], writes=[tgt_b])
        o("vector", (lambda oap=oap: lambda v: v.tensor_tensor(
            oap, oap, tstA[:], ALU.add))(),
          reads=[tstA_b, tgt_b], writes=[tgt_b])
        if nm == "wd":
            o("gpsimd" if idx % 2 == 0 else "scalar",
              (lambda k=k, slot=slot: lambda s: s.dma_start(
                  out=swdT_h[k * 128:(k + 1) * 128, :], in_=wdst[slot]))(),
              reads=[tgt_b], writes=[swdT_b[k]], dma=True)

    pb_i = [0]

    def drip_passB(n, limit=None):
        lim = len(passB) if limit is None else limit
        for _ in range(n):
            if pb_i[0] < lim:
                emit_passB(pb_i[0])
                pb_i[0] += 1

    p1_batch(0, 2)
    for t in range(NTB):
        p2_tile(t)
    o("vector", lambda v: v.tensor_copy(gk_s[:, 0:D], gkbc[:]),
      reads=[gkbc_b], writes=[gk_s_b])
    o("vector", lambda v: v.memset(gqbc[:], 0.0), writes=[gqbc_b])
    o("vector", lambda v: v.memset(gkbc[:], 0.0), writes=[gkbc_b])
    p1_batch(1, 4)
    drip_passA(len(passA))

    for idx in range(3):
        o("vector", (lambda idx=idx: lambda v: v.reduce_sum(
            wstat[:, idx:idx + 1], wsum[:, idx * 32:(idx + 1) * 32], axis=AX.X))(),
          reads=[wsum_b], writes=[wstat_b])
    o("tensor", lambda te: te.matmul(P7[:, 0:16], ones_f[:], wstat[:, 0:16],
                                     start=True, stop=True),
      reads=[ones_b, wstat_b], writes=[P7_b])
    o("vector", lambda v: v.tensor_scalar_mul(wstat[:, 4:7], P7[:, 0:3], 1.0 / (I * D)),
      reads=[P7_b], writes=[wstat_b])
    o("vector", lambda v: v.tensor_scalar_max(wstat[:, 4:7], wstat[:, 4:7], 1e-5),
      reads=[wstat_b], writes=[wstat_b])
    o("vector", lambda v: v.reciprocal(wstat[:, 8:11], wstat[:, 4:7]),
      reads=[wstat_b], writes=[wstat_b])
    o("vector", lambda v: v.tensor_scalar_mul(wstat[:, 12:15], wstat[:, 4:7], 0.5),
      reads=[wstat_b], writes=[wstat_b])

    for t in range(NTB, NT):
        p2_tile(t)
        drip_passB(6, 88)
    o("vector", lambda v: v.tensor_copy(gk_s[:, D:2 * D], gkbc[:]),
      reads=[gkbc_b], writes=[gk_s_b])
    drip_passB(len(passB))

    gate_slots = [(PB[:, 0:512], PB_bl[0]), (PB[:, 512:1024], PB_bl[1])]
    up_slots = [(PA[:, 0:512], PA_bl[0]), (PA[:, 512:1024], PA_bl[1]),
                (PC[:, 0:512], PC_bl[0]), (PC[:, 512:1024], PC_bl[1])]

    def f1A(t):
        s2 = t % 2
        o("sync", (lambda t=t, s2=s2: lambda s: s.dma_start(
            out=f_xi[s2], in_=sxi_h[t * TP:(t + 1) * TP, :]))(),
          reads=[sxi_b[t]], writes=[f_xi_bl[s2]], dma=True)

    def f1_head(t):
        s2 = t % 2

        def tr(te):
            last = None
            for k in range(8):
                last = te.transpose(PT[:, k * 128:(k + 1) * 128],
                                    f_xi[s2][:, k * 128:(k + 1) * 128], ident[:])
            return last
        o("tensor", tr, reads=[f_xi_bl[s2], ident_b], writes=[PT_b])
        o("scalar", (lambda s2=s2: lambda s: s.copy(f_xiT[s2], PT[:]))(),
          reads=[PT_b], writes=[f_xiT_bl[s2]])
        o("vector", (lambda t=t: lambda v: v.tensor_scalar(
            cgv[:, t:t + 1], xinvv[:, t:t + 1], wstat[:, 12:13], None, ALU.mult))(),
          reads=[xinvv_b, wstat_b], writes=[cgv_b])
        o("vector", (lambda t=t: lambda v: v.tensor_scalar(
            cuv[:, t:t + 1], xinvv[:, t:t + 1], wstat[:, 13:14], None, ALU.mult))(),
          reads=[xinvv_b, wstat_b], writes=[cuv_b])

    def f1_jj(t, jj):
        s2 = t % 2
        gps, gpb = gate_slots[jj % 2]
        ups, upb = up_slots[jj % 4]

        def mmjj(te, jj=jj, gps=gps, ups=ups, s2=s2):
            last = None
            for k in range(8):
                lhsT = f_xiT[s2][:, k * 128:(k + 1) * 128]
                last = te.matmul(gps, lhsT,
                                 wgT_sb[:, k * I + jj * 512:k * I + (jj + 1) * 512],
                                 start=(k == 0), stop=(k == 7))
                last = te.matmul(ups, lhsT,
                                 wuT_sb[:, k * I + jj * 512:k * I + (jj + 1) * 512],
                                 start=(k == 0), stop=(k == 7))
            return last
        o("tensor", mmjj, reads=[f_xiT_bl[s2], wgT_sb_b, wuT_sb_b],
          writes=[gpb, upb])
        o("scalar", (lambda t=t, jj=jj, gps=gps: lambda s: s.activation(
            f_g[jj % 2], gps, AF.Silu, scale=cgv[:, t:t + 1]))(),
          reads=[gpb, cgv_b], writes=[f_g_bl[jj % 2]])
        o("vector", (lambda t=t, jj=jj, ups=ups: lambda v: v.scalar_tensor_tensor(
            f_h[:, jj * 512:(jj + 1) * 512], ups, cuv[:, t:t + 1],
            f_g[jj % 2], ALU.mult, ALU.mult))(),
          reads=[upb, cuv_b, f_g_bl[jj % 2]], writes=[f_h_b])

    def f1_rowmax(t):
        slot = t % 4
        ss = slot * 8
        sb_ = stat_bl[slot]
        o("vector", (lambda ss=ss: lambda v: v.reduce_max(
            stat[:, ss + 7:ss + 8], f_h[:], axis=AX.X, apply_absolute_value=True))(),
          reads=[f_h_b], writes=[sb_])

    def f1C_head(t):
        slot = t % 4
        ss = slot * 8
        sb_ = stat_bl[slot]
        o("vector", (lambda ss=ss: lambda v: v.tensor_scalar_max(
            stat[:, ss + 7:ss + 8], stat[:, ss + 7:ss + 8], 1e-5))(),
          reads=[sb_], writes=[sb_])
        o("vector", (lambda t=t, ss=ss: lambda v: v.tensor_scalar_mul(
            hinvv[:, t:t + 1], stat[:, ss + 7:ss + 8], 1.0 / 127.0))(),
          reads=[sb_], writes=[hinvv_b])
        o("vector", (lambda t=t: lambda v: v.tensor_scalar(
            cdv[:, t:t + 1], hinvv[:, t:t + 1], wstat[:, 14:15], None, ALU.mult))(),
          reads=[hinvv_b, wstat_b], writes=[cdv_b])
        o("vector", (lambda ss=ss: lambda v: v.reciprocal(
            stat[:, ss + 7:ss + 8], stat[:, ss + 7:ss + 8]))(),
          reads=[sb_], writes=[sb_])
        o("vector", (lambda t=t, ss=ss: lambda v: v.tensor_scalar_mul(
            hsv[:, t:t + 1], stat[:, ss + 7:ss + 8], 127.0))(),
          reads=[sb_], writes=[hsv_b])

    def f1C_pair(t, p, jj_emit):
        qA, qB = 2 * p, 2 * p + 1
        hhA = f_h[:, qA * D:(qA + 1) * D]
        hhB = f_h[:, qB * D:(qB + 1) * D]
        o("vector", (lambda t=t, hhA=hhA: lambda v: v.tensor_scalar(
            fl1[:], hhA, hsv[:, t:t + 1], 128.5, ALU.mult, ALU.add))(),
          reads=[f_h_b, hsv_b], writes=[fl1_b])
        o("vector", (lambda t=t, hhB=hhB: lambda v: v.tensor_scalar(
            fl2[:], hhB, hsv[:, t:t + 1], 128.5, ALU.mult, ALU.add))(),
          reads=[f_h_b, hsv_b], writes=[fl2_b])
        for j in range(8):
            cc = float(2 ** (7 - j))
            scale = (2.0 ** 40) if j == 0 else -(2.0 ** 40)
            op1 = ALU.subtract if j == 0 else ALU.add
            o("scalar", (lambda j=j, scale=scale: lambda s: s.activation(
                rA[:], fl1[:], AF.Relu, scale=scale, bias=cbias[:, j:j + 1]))(),
              reads=[fl1_b, cbias_b], writes=[rA_b])
            o("vector", (lambda cc=cc, op1=op1: lambda v: v.scalar_tensor_tensor(
                fl1[:], rA[:], cc, fl1[:], ALU.min, op1))(),
              reads=[rA_b], writes=[fl1_b])
            o("scalar", (lambda j=j, scale=scale: lambda s: s.activation(
                rB[:], fl2[:], AF.Relu, scale=scale, bias=cbias[:, j:j + 1]))(),
              reads=[fl2_b, cbias_b], writes=[rB_b])
            o("vector", (lambda cc=cc, op1=op1: lambda v: v.scalar_tensor_tensor(
                fl2[:], rB[:], cc, fl2[:], ALU.min, op1))(),
              reads=[rB_b], writes=[fl2_b])
        o("vector", (lambda t=t, qA=qA, hhA=hhA: lambda v: v.scalar_tensor_tensor(
            f_hq[:, qA * D:(qA + 1) * D], hhA, hsv[:, t:t + 1], fl1[:],
            ALU.mult, ALU.add))(),
          reads=[f_h_b, hsv_b, fl1_b], writes=[f_hq_b])
        o("vector", (lambda t=t, qB=qB, hhB=hhB: lambda v: v.scalar_tensor_tensor(
            f_hq[:, qB * D:(qB + 1) * D], hhB, hsv[:, t:t + 1], fl2[:],
            ALU.mult, ALU.add))(),
          reads=[f_h_b, hsv_b, fl2_b], writes=[f_hq_b])
        jj_emit()

    def f1C_dma(t):
        o("sync", (lambda t=t: lambda s: s.dma_start(
            out=shqT_h[t * TP:(t + 1) * TP, :], in_=f_hq[:]))(),
          reads=[f_hq_b], writes=[shqT_b[t]], dma=True)

    f1A(0)
    f1A(1)
    f1_head(0)
    for jj in range(8):
        f1_jj(0, jj)
    f1_rowmax(0)
    for s_ in range(1, NT + 1):
        if s_ + 1 < NT:
            f1A(s_ + 1)
        if s_ < NT:
            f1_head(s_)
        f1C_head(s_ - 1)
        for p_ in range(2):
            def jj_emit(s_=s_, p_=p_):
                if s_ < NT:
                    for j4 in range(4):
                        f1_jj(s_, 4 * p_ + j4)
            f1C_pair(s_ - 1, p_, jj_emit)
        f1C_dma(s_ - 1)
        if s_ < NT:
            f1_rowmax(s_)

    for k in range(8):
        o("gpsimd", (lambda k=k: lambda g: g.dma_start(
            out=wr_sb[:, k * D:(k + 1) * D], in_=wr_h[k * 128:(k + 1) * 128, :]))(),
          writes=[wr_sb_b], dma=True)
    for k in range(32):
        o("sync", (lambda k=k: lambda s: s.dma_start(
            out=wdT_sb[:, k * D:(k + 1) * D], in_=swdT_h[k * 128:(k + 1) * 128, :]))(),
          reads=[swdT_b[k]], writes=[wdT_sb_b], dma=True)
    for h2 in range(2):
        def rsmm(te, h2=h2):
            last = None
            for k in range(32):
                last = te.matmul(P7[0:1, :], halfv[:],
                                 wdT_sb[:, k * D + h2 * 512:k * D + (h2 + 1) * 512],
                                 start=(k == 0), stop=(k == 31))
            return last
        o("tensor", rsmm, reads=[halfv_b, wdT_sb_b], writes=[P7_b])
        o("vector", (lambda h2=h2: lambda v: v.tensor_copy(
            nrs_sb[:, h2 * 512:(h2 + 1) * 512], P7[0:1, :]))(),
          reads=[P7_b], writes=[nrs_sb_b])
    for h2 in range(2):
        o("tensor", (lambda h2=h2: lambda te: te.matmul(
            P7[:], ones_f[0:1, :], nrs_sb[:, h2 * 512:(h2 + 1) * 512],
            start=True, stop=True))(),
          reads=[ones_b, nrs_sb_b], writes=[P7_b])
        o("vector", (lambda h2=h2: lambda v: v.tensor_copy(
            corr_bc[:, h2 * 512:(h2 + 1) * 512], P7[:]))(),
          reads=[P7_b], writes=[corr_bc_b])

    def f2A(t):
        s2 = t % 2
        o("sync", (lambda t=t, s2=s2: lambda s: s.dma_start(
            out=z_hq[s2], in_=shqT_h[t * TP:(t + 1) * TP, :]))(),
          reads=[shqT_b[t]], writes=[z_hq_bl[s2]], dma=True)
        o("scalar", (lambda t=t, s2=s2: lambda s: s.dma_start(
            out=z_v[s2], in_=sV_h[t * TP:(t + 1) * TP, :]))(),
          reads=[sV_b[t]], writes=[z_v_bl[s2]], dma=True)
        o("scalar", (lambda t=t: lambda s: s.dma_start(
            out=z_acc[t % 4], in_=sacc_h[t * TP:(t + 1) * TP, :]))(),
          reads=[sacc_b[t]], writes=[z_acc_bl[t % 4]], dma=True)

    def f2B(t):
        s2 = t % 2
        bi = t // NTB
        o("vector", (lambda bi=bi, s2=s2: lambda v: v.tensor_tensor(
            z_u[:], gk_s[:, bi * D:(bi + 1) * D], z_v[s2], ALU.mult))(),
          reads=[gk_s_b, z_v_bl[s2]], writes=[z_u_b])

        def tru(te):
            last = None
            for k in range(8):
                last = te.transpose(PT[:, k * 128:(k + 1) * 128],
                                    z_u[:, k * 128:(k + 1) * 128], ident[:])
            return last
        o("tensor", tru, reads=[z_u_b, ident_b], writes=[PT_b])
        o("scalar", (lambda s2=s2: lambda s: s.copy(z_uT[s2], PT[:]))(),
          reads=[PT_b], writes=[z_uT_bl[s2]])
        for r in range(4):
            def trh(te, r=r, s2=s2):
                last = None
                for k in range(8):
                    c0 = r * 1024 + k * 128
                    last = te.transpose(PT[:, k * 128:(k + 1) * 128],
                                        z_hq[s2][:, c0:c0 + 128], ident[:])
                return last
            o("tensor", trh, reads=[z_hq_bl[s2], ident_b], writes=[PT_b])
            o("scalar", (lambda r=r, s2=s2: lambda s: s.copy(
                z_hqT[s2][:, r * 1024:(r + 1) * 1024], PT[:]))(),
              reads=[PT_b], writes=[z_hqT_bl[s2]])

    def f2C(t):
        s2 = t % 2

        def rmm(te, s2=s2):
            last = None
            for k in range(8):
                lhsT = z_uT[s2][:, k * 128:(k + 1) * 128]
                for h in range(2):
                    last = te.matmul(PB[:, h * 512:(h + 1) * 512], lhsT,
                                     wr_sb[:, k * D + h * 512:k * D + (h + 1) * 512],
                                     start=(k == 0), stop=(k == 7))
            return last
        o("tensor", rmm, reads=[z_uT_bl[s2], wr_sb_b], writes=PB_bl)

        dps = PA if (t % 2 == 0) else PC
        dpb = PA_bl if (t % 2 == 0) else PC_bl

        def dmm(te, dps=dps, s2=s2):
            last = None
            for k in range(32):
                lhsT = z_hqT[s2][:, k * 128:(k + 1) * 128]
                for h in range(2):
                    last = te.matmul(dps[:, h * 512:(h + 1) * 512], lhsT,
                                     wdT_sb[:, k * D + h * 512:k * D + (h + 1) * 512],
                                     start=(k == 0), stop=(k == 31))
            return last
        o("tensor", dmm, reads=[z_hqT_bl[s2], wdT_sb_b], writes=dpb)

    def f2D(t):
        s2 = t % 2
        dps = PA if (t % 2 == 0) else PC
        dpb = PA_bl if (t % 2 == 0) else PC_bl
        o("vector", (lambda t=t: lambda v: v.tensor_tensor(
            z_a2[:], z_acc[t % 4], PB[:], ALU.add))(),
          reads=[z_acc_bl[t % 4]] + PB_bl, writes=[z_a2_b])
        o("vector", (lambda t=t: lambda v: v.scalar_tensor_tensor(
            z_a2[:], corr_bc[:], cdv[:, t:t + 1], z_a2[:], ALU.mult, ALU.add))(),
          reads=[corr_bc_b, cdv_b, z_a2_b], writes=[z_a2_b])
        for h in range(2):
            o("vector", (lambda t=t, h=h, dps=dps, s2=s2: lambda v: v.scalar_tensor_tensor(
                z_y[s2][:, h * 512:(h + 1) * 512], dps[:, h * 512:(h + 1) * 512],
                cdv[:, t:t + 1], z_a2[:, h * 512:(h + 1) * 512], ALU.mult, ALU.add))(),
              reads=[dpb[h], cdv_b, z_a2_b], writes=[z_y_bl[s2]])
        o("sync", (lambda t=t, s2=s2: lambda s: s.dma_start(
            out=y_h[t * TP:(t + 1) * TP, :], in_=z_y[s2]))(),
          reads=[z_y_bl[s2]], writes=[], dma=True)

    f2A(0)
    f2A(1)
    for s_ in range(NT + 2):
        if s_ < NT:
            f2B(s_)
        if s_ >= 2:
            f2D(s_ - 2)
        if 1 <= s_ <= NT:
            f2C(s_ - 1)
        if s_ + 2 < NT:
            f2A(s_ + 2)

    from contextlib import ExitStack
    with ExitStack() as st:
        semh = {s: st.enter_context(nc.semaphore("sem_" + s)) for s in ALL_SEMS}
        block = st.enter_context(nc.Block())

        def mk(engname):
            def _f(eng):
                for waits, build, sem, amt in pr.items[engname]:
                    for s_, v_ in sorted(waits.items()):
                        eng.wait_ge(semh[s_], v_)
                    inst = build(eng)
                    inst.then_inc(semh[sem], amt)
                if engname == "sync":
                    for ln in DMA_LANES["DS"]:
                        if pr.counts[ln]:
                            eng.wait_ge(semh[ln], pr.counts[ln])
            return _f

        block.sync(mk("sync"))
        block.gpsimd(mk("gpsimd"))
        block.vector(mk("vector"))
        block.scalar(mk("scalar"))
        block.tensor(mk("tensor"))

    return nc


_NC_CACHE = []


def kernel(L0_attn_x, L0_attn_Wq, L0_attn_Wk, L0_attn_Wv, L0_attn_Wr,
           L0_ffn_w_gate, L0_ffn_w_up, L0_ffn_w_down):
    x = np.ascontiguousarray(np.asarray(L0_attn_x, dtype=np.float32))
    wq = np.ascontiguousarray(np.asarray(L0_attn_Wq, dtype=np.float32))
    wk = np.ascontiguousarray(np.asarray(L0_attn_Wk, dtype=np.float32))
    wv = np.ascontiguousarray(np.asarray(L0_attn_Wv, dtype=np.float32))
    wr = np.ascontiguousarray(np.asarray(L0_attn_Wr, dtype=np.float32))
    wgT = np.ascontiguousarray(np.asarray(L0_ffn_w_gate, dtype=np.float32).T)
    wuT = np.ascontiguousarray(np.asarray(L0_ffn_w_up, dtype=np.float32).T)
    wdT = np.ascontiguousarray(np.asarray(L0_ffn_w_down, dtype=np.float32).T)
    if not _NC_CACHE:
        _NC_CACHE.append(build_nc())
    nc = _NC_CACHE[0]
    in_maps = []
    for c in range(8):
        in_maps.append({
            "x": np.ascontiguousarray(x[2 * c:2 * c + 2].reshape(NT * TP, D)),
            "wq": wq, "wk": wk, "wv": wv, "wr": wr,
            "wgT": wgT, "wuT": wuT, "wdT": wdT,
        })
    res = run_bass_kernel_spmd(nc, in_maps, core_ids=list(range(8)))
    out = np.concatenate([r["y"].reshape(B, S, D) for r in res.results], axis=0)
    return out


# version 25
# speedup vs baseline: 1.0380x; 1.0380x over previous
import numpy as np
import concourse.bass as bass
import concourse.mybir as mybir
from concourse.bass_utils import run_bass_kernel_spmd

F32 = mybir.dt.float32
BF16 = mybir.dt.bfloat16
AF = mybir.ActivationFunctionType
ALU = mybir.AluOpType
AX = mybir.AxisListType

B = 2
S = 2048
D = 1024
I = 4096
TP = 128
NTB = S // TP
NT = B * NTB
SCALE = float(D) ** -0.5
EPS = 1e-5
KB = 1024

ENG_SEM = {"tensor": "T", "vector": "V", "scalar": "S", "gpsimd": "G"}
DMA_SEM = {"sync": "DS", "scalar": "DC", "gpsimd": "DG"}
NLANES = 6
DMA_LANES = {q: ["%s%d" % (q, i) for i in range(NLANES)] for q in ["DS", "DC", "DG"]}
ALL_SEMS = (["T", "V", "S", "G"]
            + DMA_LANES["DS"] + DMA_LANES["DC"] + DMA_LANES["DG"])


class Buf:
    __slots__ = ("writer", "readers", "aliases", "name")

    def __init__(self, name=""):
        self.writer = None
        self.readers = {}
        self.aliases = []
        self.name = name


class Prog:
    def __init__(self):
        self.counts = {s: 0 for s in ALL_SEMS}
        self.items = {e: [] for e in ["tensor", "vector", "scalar", "gpsimd", "sync"]}
        self.lastw = {}
        self.rr = {q: 0 for q in ["DS", "DC", "DG"]}

    def op(self, eng, build, reads=(), writes=(), dma=False):
        if dma:
            q = DMA_SEM[eng]
            sem = DMA_LANES[q][self.rr[q] % NLANES]
            self.rr[q] += 1
        else:
            sem = ENG_SEM[eng]
        amt = 16 if dma else 1
        waits = {}

        def need(mark):
            if not mark:
                return
            s, v = mark
            if s == "T" and eng == "tensor" and not dma:
                return
            if self.lastw.get((eng, s), 0) >= v:
                return
            if waits.get(s, 0) < v:
                waits[s] = v

        def need_all(b):
            need(b.writer)
            for s, v in b.readers.items():
                need((s, v))

        if dma and self.counts[sem] > 0:
            need((sem, self.counts[sem]))
        for b in reads:
            need(b.writer)
        for b in writes:
            need_all(b)
            for ab in b.aliases:
                need_all(ab)
        self.counts[sem] += amt
        mark = (sem, self.counts[sem])
        for s, v in waits.items():
            if self.lastw.get((eng, s), 0) < v:
                self.lastw[(eng, s)] = v
        self.items[eng].append((waits, build, sem, amt))
        for b in reads:
            if b.readers.get(sem, 0) < mark[1]:
                b.readers[sem] = mark[1]
        for b in writes:
            b.writer = mark
            b.readers = {}
        return mark


def _bpp(shape, dtype):
    n = 1
    for d in shape[1:]:
        n *= d
    return n * mybir.dt.size(dtype)


def build_nc():
    nc = bass.Bass("TRN2", target_bir_lowering=False)
    pr = Prog()
    o = pr.op

    x_h = nc.dram_tensor("x", [NT * TP, D], F32, kind="ExternalInput").ap()
    wq_h = nc.dram_tensor("wq", [D, D], F32, kind="ExternalInput").ap()
    wk_h = nc.dram_tensor("wk", [D, D], F32, kind="ExternalInput").ap()
    wv_h = nc.dram_tensor("wv", [D, D], F32, kind="ExternalInput").ap()
    wr_h = nc.dram_tensor("wr", [D, D], F32, kind="ExternalInput").ap()
    wgT_h = nc.dram_tensor("wgT", [D, I], F32, kind="ExternalInput").ap()
    wuT_h = nc.dram_tensor("wuT", [D, I], F32, kind="ExternalInput").ap()
    wdT_h = nc.dram_tensor("wdT", [I, D], F32, kind="ExternalInput").ap()
    y_h = nc.dram_tensor("y", [NT * TP, D], F32, kind="ExternalOutput").ap()
    sacc_h = nc.dram_tensor("sacc", [NT * TP, D], F32, kind="Internal").ap()
    sxi_h = nc.dram_tensor("sxi", [NT * TP, D], BF16, kind="Internal").ap()
    sV_h = nc.dram_tensor("sV", [NT * TP, D], BF16, kind="Internal").ap()
    shqT_h = nc.dram_tensor("shqT", [NT * TP, I], BF16, kind="Internal").ap()
    swdT_h = nc.dram_tensor("swdT", [I, D], BF16, kind="Internal").ap()

    sacc_b = [Buf("sacc%d" % t) for t in range(NT)]
    sxi_b = [Buf("sxi%d" % t) for t in range(NT)]
    sV_b = [Buf("sV%d" % t) for t in range(NT)]
    shqT_b = [Buf("shqT%d" % t) for t in range(NT)]
    swdT_b = [Buf("swdT%d" % k) for k in range(32)]

    allocs = []

    def sb(name, shape, dtype, off_kb, nbuf=1):
        off = int(off_kb * KB)
        h = nc.alloc_sbuf_tensor_at(name, shape, dtype, offset=off)
        end = off + _bpp(shape, dtype)
        assert end <= int(207.8 * KB), (name, end)
        bufs = Buf(name) if nbuf == 1 else [Buf("%s%d" % (name, i)) for i in range(nbuf)]
        bl = bufs if isinstance(bufs, list) else [bufs]
        for (s0, e0, bl0) in allocs:
            if s0 < end and off < e0:
                for bb in bl:
                    for ob in bl0:
                        bb.aliases.append(ob)
                        ob.aliases.append(bb)
        allocs.append((off, end, bl))
        return h.ap(), bufs

    ident, ident_b = sb("ident", [128, 128], BF16, 18.0)
    ones_f, ones_b = sb("ones_f", [128, 128], F32, 18.25)
    svec, svec_b = sb("svec", [128, NT], F32, 18.75)
    xinvv, xinvv_b = sb("xinvv", [128, NT], F32, 18.875)
    hinvv, hinvv_b = sb("hinvv", [128, NT], F32, 19.0)
    cgv, cgv_b = sb("cgv", [128, NT], F32, 19.125)
    cuv, cuv_b = sb("cuv", [128, NT], F32, 19.25)
    cdv, cdv_b = sb("cdv", [128, NT], F32, 19.375)
    hsv, hsv_b = sb("hsv", [128, NT], F32, 19.5)
    epsv, epsv_b = sb("epsv", [128, 1], F32, 19.625)
    cbias, cbias_b = sb("cbias", [128, 8], F32, 19.75)
    halfv, halfv_b = sb("halfv", [128, 1], BF16, 19.8125)
    bm5, bm5_b = sb("bm5", [128, 1], F32, 19.875)
    bp5, bp5_b = sb("bp5", [128, 1], F32, 19.9375)
    wsum, wsum_b = sb("wsum", [128, 96], F32, 20.0)
    wstat, wstat_b = sb("wstat", [128, 16], F32, 20.375)
    stat, stat_bl = sb("stat", [128, 32], F32, 20.4375, nbuf=4)
    gk_s, gk_s_b = sb("gk_s", [128, 2 * D], F32, 20.625)
    gqbc, gqbc_b = sb("gqbc", [128, D], F32, 28.625)
    gkbc, gkbc_b = sb("gkbc", [128, D], F32, 32.625)
    iotat, iotat_b = sb("iotat", [128, 128], F32, 37.125)
    Kst, Kst_bl = sb("Kst", [128, NTB * D], BF16, 37.125, nbuf=NTB)
    wq_sb, wq_sb_b = sb("wq_sb", [128, 8 * D], BF16, 69.125)
    wk_sb, wk_sb_b = sb("wk_sb", [128, 8 * D], BF16, 85.125)
    wv_sb, wv_sb_b = sb("wv_sb", [128, 8 * D], BF16, 101.125)
    NXB = 6
    xbuf_t, xbuf_bl4 = sb("xbuf", [128, 4 * D], F32, 117.125, nbuf=4)
    xbuf2_t, xbuf_bl2 = sb("xbuf2", [128, 2 * D], F32, 171.375, nbuf=2)
    xbuf = ([xbuf_t[:, i * D:(i + 1) * D] for i in range(4)]
            + [xbuf2_t[:, i * D:(i + 1) * D] for i in range(2)])
    xbuf_bl = xbuf_bl4 + xbuf_bl2
    xnf_t, xnf_bl = sb("xnf", [128, D], F32, 133.125, nbuf=1)
    xnf2_t, xnf2_bl = sb("xnf2", [128, D], F32, 179.375, nbuf=1)
    xnf_s = [xnf_t, xnf2_t]
    xnf_sbl = [xnf_bl, xnf2_bl]
    sqs, sqs_b = sb("sqs", [128, D], F32, 137.125)
    xi16b_t, xi16b_bl = sb("xi16b", [128, D], BF16, 183.375)
    accs2_t, accs2_bl = sb("accs2", [128, D], F32, 185.375)
    v16b_t, v16b_bl = sb("v16b", [128, D], BF16, 189.375)
    e_sb, e_sb_b = sb("e_sb", [128, D], F32, 141.125)
    accs, accs_b = sb("accs", [128, D], F32, 145.125)
    lh1, lh1_b = sb("lh1", [128, D], F32, 149.125)
    lh2, lh2_b = sb("lh2", [128, D], F32, 153.125)
    xn16_t, xn16_bl = sb("xn16", [128, 2 * D], BF16, 157.125, nbuf=2)
    xn16 = [xn16_t[:, i * D:(i + 1) * D] for i in range(2)]
    xnT_t, xnT_bl = sb("xnT_s", [128, 2 * D], BF16, 161.125, nbuf=2)
    xnT_s = [xnT_t[:, i * D:(i + 1) * D] for i in range(2)]
    xi16, xi16_b = sb("xi16", [128, D], BF16, 165.125)
    v16, v16_b = sb("v16", [128, D], BF16, 167.125)
    w2_s, w2_s_b = sb("w2_s", [128, D], BF16, 169.125)
    rep_s, rep_s_b = sb("rep_s", [128, 128], BF16, 171.125)
    wf32_t, wf32_bl = sb("wf32", [128, 2 * D], F32, 197.375, nbuf=2)
    wf32 = [wf32_t[:, i * D:(i + 1) * D] for i in range(2)]
    wabs, wabs_b = sb("wabs", [128, D], BF16, 205.375)
    p2_pbuf, p2_pbuf_b = sb("p2_pbuf", [128, D], F32, 117.15625)
    p2_e, p2_e_b = sb("p2_e", [128, D], F32, 121.15625)
    p2_w2, p2_w2_b = sb("p2_w2", [128, D], BF16, 125.15625)
    p2_rep, p2_rep_b = sb("p2_rep", [128, 128], BF16, 127.15625)
    tstA, tstA_b = sb("tstA", [128, D], BF16, 205.375)
    wdst_t, wdst_bl = sb("wdst", [128, 2 * D], BF16, 69.125, nbuf=2)
    wdst = [wdst_t[:, i * D:(i + 1) * D] for i in range(2)]
    wf32x_t, wf32x_bl = sb("wf32x", [128, 3 * D], F32, 37.125, nbuf=3)
    wgT_sb, wgT_sb_b = sb("wgT_sb", [128, 8 * I], BF16, 69.125)
    wuT_sb, wuT_sb_b = sb("wuT_sb", [128, 8 * I], BF16, 133.125)
    f_xi_t, f_xi_bl = sb("f_xi", [128, 2 * D], BF16, 37.125, nbuf=2)
    f_xi = [f_xi_t[:, i * D:(i + 1) * D] for i in range(2)]
    f_xiT_t, f_xiT_bl = sb("f_xiT", [128, 2 * D], BF16, 41.125, nbuf=2)
    f_xiT = [f_xiT_t[:, i * D:(i + 1) * D] for i in range(2)]
    f_g_t, f_g_bl = sb("f_g", [128, 2 * 512], F32, 45.125, nbuf=2)
    f_g = [f_g_t[:, i * 512:(i + 1) * 512] for i in range(2)]
    f_h, f_h_b = sb("f_h", [128, I], F32, 49.125)
    fl1, fl1_b = sb("fl1", [128, D], F32, 28.625)
    fl2, fl2_b = sb("fl2", [128, D], F32, 32.625)
    rA, rA_b = sb("rA", [128, D], BF16, 65.125)
    rB, rB_b = sb("rB", [128, D], BF16, 67.125)
    f_hq, f_hq_b = sb("f_hq", [128, I], BF16, 197.375)
    wr_sb, wr_sb_b = sb("wr_sb", [128, 8 * D], BF16, 37.125)
    wdT_sb, wdT_sb_b = sb("wdT_sb", [128, 32 * D], BF16, 133.125)
    z_v_t, z_v_bl = sb("z_v", [128, 2 * D], BF16, 53.125, nbuf=2)
    z_v = [z_v_t[:, i * D:(i + 1) * D] for i in range(2)]
    z_u, z_u_b = sb("z_u", [128, D], BF16, 57.125)
    z_uT_t, z_uT_bl = sb("z_uT", [128, 2 * D], BF16, 59.125, nbuf=2)
    z_uT = [z_uT_t[:, i * D:(i + 1) * D] for i in range(2)]
    z_a2, z_a2_b = sb("z_a2", [128, D], F32, 63.125)
    nrs_sb, nrs_sb_b = sb("nrs_sb", [1, D], F32, 125.125)
    z_hq_t, z_hq_bl = sb("z_hq", [128, 2 * I], BF16, 69.125, nbuf=2)
    z_hq = [z_hq_t[:, i * I:(i + 1) * I] for i in range(2)]
    z_hqT_t, z_hqT_bl = sb("z_hqT", [128, 2 * I], BF16, 85.125, nbuf=2)
    z_hqT = [z_hqT_t[:, i * I:(i + 1) * I] for i in range(2)]
    z_acc_t, z_acc_bl = sb("z_acc", [128, 4 * D], F32, 101.125, nbuf=4)
    z_acc = [z_acc_t[:, i * D:(i + 1) * D] for i in range(4)]
    z_y_t, z_y_bl = sb("z_y", [128, 2 * D], F32, 117.125, nbuf=2)
    z_y = [z_y_t[:, i * D:(i + 1) * D] for i in range(2)]
    corr_bc, corr_bc_b = sb("corr_bc", [128, D], F32, 28.625)

    PA_h = nc.alloc_psum_tensor("PA", [128, D], F32)
    PB_h = nc.alloc_psum_tensor("PB", [128, D], F32)
    PC_h = nc.alloc_psum_tensor("PC", [128, D], F32)
    PT_h = nc.alloc_psum_tensor("PT", [128, 2 * D], BF16)
    PA, PB, PC, PT = PA_h.ap(), PB_h.ap(), PC_h.ap(), PT_h.ap()
    P7 = PB[:, 0:512]
    PTf = PT.bitcast(F32)
    PA_bl = [Buf("PA0"), Buf("PA1")]
    PB_bl = [Buf("PB0"), Buf("PB1")]
    PC_bl = [Buf("PC0"), Buf("PC1")]
    PT_b = Buf("PT")
    PTX = [PT[:, 0:D], PT[:, D:2 * D]]
    P7_b = Buf("P7")
    PTX_bl = [Buf("PTX0"), Buf("PTX1")]
    P7_b.aliases.append(PB_bl[0])
    PB_bl[0].aliases.append(P7_b)
    trrot = [0]

    def tr_pingpong(src_ap, src_b, dst_ap, dst_b, src_off=0):
        hb = trrot[0] % 2
        trrot[0] += 1
        buf_ap, buf_b = PTX[hb], PTX_bl[hb]

        def fill(te, buf_ap=buf_ap, src_ap=src_ap, src_off=src_off):
            last = None
            for k in range(8):
                last = te.transpose(buf_ap[:, k * 128:(k + 1) * 128],
                                    src_ap[:, src_off + k * 128:src_off + (k + 1) * 128],
                                    ident[:])
            return last
        o("tensor", fill, reads=[src_b, ident_b], writes=[buf_b])
        o("scalar", (lambda buf_ap=buf_ap, dst_ap=dst_ap:
                     lambda s: s.copy(dst_ap[:, 0:D], buf_ap[:, 0:D]))(),
          reads=[buf_b], writes=[dst_b])

    o("gpsimd", lambda g: g.iota(iotat[:], [[1, 128]], channel_multiplier=-1,
                                 allow_small_or_imprecise_dtypes=True),
      writes=[iotat_b])
    o("vector", lambda v: v.tensor_single_scalar(ident[:], iotat[:], 0.0, ALU.is_equal),
      reads=[iotat_b], writes=[ident_b])
    o("vector", lambda v: v.memset(ones_f[:], 1.0), writes=[ones_b])
    o("vector", lambda v: v.memset(gqbc[:], 0.0), writes=[gqbc_b])
    o("vector", lambda v: v.memset(gkbc[:], 0.0), writes=[gkbc_b])
    o("vector", lambda v: v.memset(wsum[:], 0.0), writes=[wsum_b])
    o("vector", lambda v: v.memset(wstat[:], 0.0), writes=[wstat_b])
    o("vector", lambda v: v.memset(epsv[:], EPS), writes=[epsv_b])
    o("vector", lambda v: v.memset(halfv[:], 0.5), writes=[halfv_b])
    o("vector", lambda v: v.memset(bm5[:], -0.5), writes=[bm5_b])
    o("vector", lambda v: v.memset(bp5[:], 0.5), writes=[bp5_b])
    for j in range(8):
        cval = -float(2 ** (7 - j)) * (2.0 ** 40)
        o("vector", (lambda j=j, cval=cval: lambda v: v.memset(
            cbias[:, j:j + 1], cval))(), writes=[cbias_b])

    for w_hh, w_sbb, w_bb in [(wq_h, wq_sb, wq_sb_b), (wk_h, wk_sb, wk_sb_b),
                              (wv_h, wv_sb, wv_sb_b)]:
        for k in range(8):
            o("gpsimd",
              (lambda w_hh=w_hh, w_sbb=w_sbb, k=k:
               lambda g: g.dma_start(out=w_sbb[:, k * D:(k + 1) * D],
                                     in_=w_hh[k * 128:(k + 1) * 128, :]))(),
              writes=[w_bb], dma=True)

    passA = ([("wg", k) for k in range(32)] + [("wu", k) for k in range(32)]
             + [("wd", k) for k in range(32)])

    def emit_passA(idx):
        nm, k = passA[idx]
        slot = idx % 2
        if nm == "wd":
            src = wdT_h[k * 128:(k + 1) * 128, :]
        else:
            w_hh = wgT_h if nm == "wg" else wuT_h
            src = w_hh[(k // 4) * 128:(k // 4 + 1) * 128, (k % 4) * D:(k % 4 + 1) * D]
        col = {"wg": 0, "wu": 32, "wd": 64}[nm] + k
        o("scalar", (lambda src=src, slot=slot:
                     lambda s: s.dma_start(out=wf32[slot], in_=src))(),
          writes=[wf32_bl[slot]], dma=True)
        o("scalar", (lambda slot=slot, col=col:
                     lambda s: s.activation(wabs[:], wf32[slot], AF.Abs,
                                            accum_out=wsum[:, col:col + 1]))(),
          reads=[wf32_bl[slot]], writes=[wabs_b, wsum_b])

    pa_i = [0]

    def drip_passA(n):
        for _ in range(n):
            if pa_i[0] < len(passA):
                emit_passA(pa_i[0])
                pa_i[0] += 1

    def stA(t):
        slot = t % NXB
        o("sync", (lambda t=t, slot=slot:
                   lambda s: s.dma_start(out=xbuf[slot], in_=x_h[t * TP:(t + 1) * TP, :]))(),
          writes=[xbuf_bl[slot]], dma=True)

    def stB(t):
        slot = t % 4
        ss = slot * 8
        sb_ = stat_bl[slot]
        xb = xbuf[t % NXB]
        xbb = xbuf_bl[t % NXB]
        c = lambda i: stat[:, ss + i:ss + i + 1]
        o("vector", (lambda xb=xb, ss=ss: lambda v: v.reduce_sum(
            stat[:, ss:ss + 1], xb, axis=AX.X))(), reads=[xbb], writes=[sb_])
        o("vector", (lambda ss=ss: lambda v: v.tensor_scalar_mul(
            stat[:, ss + 1:ss + 2], stat[:, ss:ss + 1], 1.0 / D))(),
          reads=[sb_], writes=[sb_])
        o("scalar", (lambda xb=xb, ss=ss: lambda s: s.activation(
            sqs[:], xb, AF.Square, accum_out=stat[:, ss + 2:ss + 3]))(),
          reads=[xbb], writes=[sqs_b, sb_])
        o("vector", (lambda ss=ss: lambda v: v.tensor_scalar_mul(
            stat[:, ss + 2:ss + 3], stat[:, ss + 2:ss + 3], 1.0 / D))(),
          reads=[sb_], writes=[sb_])
        o("vector", (lambda ss=ss: lambda v: v.scalar_tensor_tensor(
            stat[:, ss + 3:ss + 4], stat[:, ss + 1:ss + 2], stat[:, ss + 1:ss + 2],
            stat[:, ss + 2:ss + 3], ALU.mult, ALU.subtract))(),
          reads=[sb_], writes=[sb_])
        o("scalar", (lambda ss=ss: lambda s: s.activation(
            stat[:, ss + 4:ss + 5], stat[:, ss + 3:ss + 4], AF.Ln,
            scale=-1.0, bias=epsv[:, 0:1]))(), reads=[sb_, epsv_b], writes=[sb_])
        o("scalar", (lambda ss=ss: lambda s: s.activation(
            stat[:, ss + 5:ss + 6], stat[:, ss + 4:ss + 5], AF.Exp,
            scale=-0.5))(), reads=[sb_], writes=[sb_])
        xnf = xnf_s[t % 2]
        xnf_b = xnf_sbl[t % 2]
        xi16_, xi16_b_ = (xi16, xi16_b) if t % 2 == 0 else (xi16b_t, xi16b_bl)
        o("vector", (lambda xb=xb, ss=ss, xnf=xnf: lambda v: v.tensor_scalar(
            xnf[:], xb, stat[:, ss + 1:ss + 2], stat[:, ss + 5:ss + 6],
            ALU.subtract, ALU.mult))(),
          reads=[xbb, sb_], writes=[xnf_b])
        o("vector", (lambda ss=ss, xnf=xnf: lambda v: v.reduce_max(
            stat[:, ss + 7:ss + 8], xnf[:], axis=AX.X, apply_absolute_value=True))(),
          reads=[xnf_b], writes=[sb_])
        o("vector", (lambda ss=ss: lambda v: v.tensor_scalar_max(
            stat[:, ss + 7:ss + 8], stat[:, ss + 7:ss + 8], 1e-5))(),
          reads=[sb_], writes=[sb_])
        o("vector", (lambda t=t, ss=ss: lambda v: v.tensor_scalar_mul(
            xinvv[:, t:t + 1], stat[:, ss + 7:ss + 8], 1.0 / 127.0))(),
          reads=[sb_], writes=[xinvv_b])
        o("vector", (lambda ss=ss: lambda v: v.reciprocal(
            stat[:, ss + 7:ss + 8], stat[:, ss + 7:ss + 8]))(),
          reads=[sb_], writes=[sb_])
        o("vector", (lambda t=t, ss=ss: lambda v: v.tensor_scalar_mul(
            svec[:, t:t + 1], stat[:, ss + 7:ss + 8], 127.0))(),
          reads=[sb_], writes=[svec_b])
        o("scalar", (lambda t=t, xnf=xnf: lambda s: s.copy(xn16[t % 2], xnf[:]))(),
          reads=[xnf_b], writes=[xn16_bl[t % 2]])
        o("vector", (lambda t=t, xnf=xnf: lambda v: v.tensor_scalar(
            lh1[:], xnf[:], svec[:, t:t + 1], 128.5, ALU.mult, ALU.add))(),
          reads=[xnf_b, svec_b], writes=[lh1_b])
        for kb in range(7, -1, -1):
            cc = float(2 ** kb)
            o("vector", (lambda cc=cc: lambda v: v.tensor_scalar(
                lh2[:], lh1[:], cc, cc, ALU.is_ge, ALU.mult))(),
              reads=[lh1_b], writes=[lh2_b])
            o("vector", lambda v: v.tensor_tensor(lh1[:], lh1[:], lh2[:], ALU.subtract),
              reads=[lh2_b], writes=[lh1_b])
        o("vector", (lambda t=t, xnf=xnf: lambda v: v.tensor_scalar(
            lh2[:], xnf[:], svec[:, t:t + 1], 0.5, ALU.mult, ALU.add))(),
          reads=[xnf_b, svec_b], writes=[lh2_b])
        o("vector", (lambda xi16_=xi16_: lambda v: v.tensor_tensor(
            xi16_[:], lh2[:], lh1[:], ALU.subtract))(),
          reads=[lh2_b, lh1_b], writes=[xi16_b_])
        o("gpsimd", (lambda t=t, xi16_=xi16_: lambda g: g.dma_start(
            out=sxi_h[t * TP:(t + 1) * TP, :], in_=xi16_[:]))(),
          reads=[xi16_b_], writes=[sxi_b[t]], dma=True)

    def stC(t):
        s2 = t % 2

        tr_pingpong(xn16[s2], xn16_bl[s2], xnT_s[s2], xnT_bl[s2])

        def qkv(te):
            last = None
            for k in range(8):
                lhsT = xnT_s[s2][:, k * 128:(k + 1) * 128]
                for (ws, ps) in [(wq_sb, PA), (wk_sb, PB), (wv_sb, PC)]:
                    for h in range(2):
                        last = te.matmul(ps[:, h * 512:(h + 1) * 512], lhsT,
                                         ws[:, k * D + h * 512:k * D + (h + 1) * 512],
                                         start=(k == 0), stop=(k == 7))
            return last
        o("tensor", qkv, reads=[xnT_bl[s2], wq_sb_b, wk_sb_b, wv_sb_b],
          writes=PA_bl + PB_bl + PC_bl)

    def stD(t):
        slot = t % 4
        ss = slot * 8
        sb_ = stat_bl[slot]
        tb = t % NTB
        o("scalar", (lambda tb=tb: lambda s: s.copy(
            Kst[:, tb * D:(tb + 1) * D], PB[:]))(),
          reads=PB_bl, writes=[Kst_bl[tb]])
        v16_, v16_b_ = (v16, v16_b) if t % 2 == 0 else (v16b_t, v16b_bl)
        o("scalar", (lambda v16_=v16_: lambda s: s.copy(v16_[:], PC[:]))(),
          reads=PC_bl, writes=[v16_b_])
        o("scalar", (lambda t=t, v16_=v16_: lambda s: s.dma_start(
            out=sV_h[t * TP:(t + 1) * TP, :], in_=v16_[:]))(),
          reads=[v16_b_], writes=[sV_b[t]], dma=True)
        o("vector", (lambda ss=ss: lambda v: v.reduce_max(
            stat[:, ss + 7:ss + 8], PA[:], axis=AX.X))(),
          reads=PA_bl, writes=[sb_])
        o("scalar", (lambda ss=ss: lambda s: s.mul(
            stat[:, ss + 6:ss + 7], stat[:, ss + 7:ss + 8], -SCALE))(),
          reads=[sb_], writes=[sb_])
        o("scalar", (lambda ss=ss: lambda s: s.activation(
            e_sb[:], PA[:], AF.Exp, bias=stat[:, ss + 6:ss + 7], scale=SCALE,
            accum_out=stat[:, ss + 5:ss + 6]))(),
          reads=PA_bl + [sb_], writes=[e_sb_b, sb_])
        o("vector", (lambda ss=ss: lambda v: v.reciprocal(
            stat[:, ss + 4:ss + 5], stat[:, ss + 5:ss + 6]))(),
          reads=[sb_], writes=[sb_])
        o("vector", (lambda ss=ss: lambda v: v.tensor_scalar(
            rep_s[:], ones_f[:], stat[:, ss + 4:ss + 5], None, ALU.mult))(),
          reads=[ones_b, sb_], writes=[rep_s_b])
        o("vector", lambda v: v.tensor_tensor(w2_s[:], PA[:], e_sb[:], ALU.mult),
          reads=PA_bl + [e_sb_b], writes=[w2_s_b])
        accs_, accs_b_ = (accs, accs_b) if t % 2 == 0 else (accs2_t, accs2_bl)
        xslot = t % NXB
        o("vector", (lambda xslot=xslot, accs_=accs_: lambda v: v.tensor_tensor(
            accs_[:], xbuf[xslot], PA[:], ALU.add))(),
          reads=[xbuf_bl[xslot]] + PA_bl, writes=[accs_b_])
        o("sync", (lambda t=t, accs_=accs_: lambda s: s.dma_start(
            out=sacc_h[t * TP:(t + 1) * TP, 0:512], in_=accs_[:, 0:512]))(),
          reads=[accs_b_], writes=[sacc_b[t]], dma=True)
        o("scalar", (lambda t=t, accs_=accs_: lambda s: s.dma_start(
            out=sacc_h[t * TP:(t + 1) * TP, 512:D], in_=accs_[:, 512:D]))(),
          reads=[accs_b_], writes=[sacc_b[t]], dma=True)
        for h in range(2):
            o("tensor", (lambda h=h: lambda te: te.matmul(
                P7[:], rep_s[:], w2_s[:, h * 512:(h + 1) * 512],
                start=True, stop=True))(),
              reads=[rep_s_b, w2_s_b], writes=[P7_b])
            o("vector", (lambda h=h: lambda v: v.tensor_tensor(
                gqbc[:, h * 512:(h + 1) * 512], gqbc[:, h * 512:(h + 1) * 512],
                P7[:], ALU.add))(),
              reads=[P7_b], writes=[gqbc_b])

    def p2_tile(t):
        slot = t % 4
        ss = slot * 8
        sb_ = stat_bl[slot]
        tb = t % NTB
        o("vector", (lambda tb=tb: lambda v: v.tensor_tensor(
            p2_pbuf[:], gqbc[:], Kst[:, tb * D:(tb + 1) * D], ALU.mult))(),
          reads=[gqbc_b, Kst_bl[tb]], writes=[p2_pbuf_b])
        o("vector", (lambda ss=ss: lambda v: v.reduce_max(
            stat[:, ss + 7:ss + 8], p2_pbuf[:], axis=AX.X))(),
          reads=[p2_pbuf_b], writes=[sb_])
        o("scalar", (lambda ss=ss: lambda s: s.mul(
            stat[:, ss + 6:ss + 7], stat[:, ss + 7:ss + 8], -SCALE))(),
          reads=[sb_], writes=[sb_])
        o("scalar", (lambda ss=ss: lambda s: s.activation(
            p2_e[:], p2_pbuf[:], AF.Exp, bias=stat[:, ss + 6:ss + 7], scale=SCALE,
            accum_out=stat[:, ss + 5:ss + 6]))(),
          reads=[p2_pbuf_b, sb_], writes=[p2_e_b, sb_])
        o("vector", (lambda ss=ss: lambda v: v.reciprocal(
            stat[:, ss + 4:ss + 5], stat[:, ss + 5:ss + 6]))(),
          reads=[sb_], writes=[sb_])
        o("vector", (lambda ss=ss: lambda v: v.tensor_scalar(
            p2_rep[:], ones_f[:], stat[:, ss + 4:ss + 5], None, ALU.mult))(),
          reads=[ones_b, sb_], writes=[p2_rep_b])
        o("vector", lambda v: v.tensor_tensor(p2_w2[:], p2_pbuf[:], p2_e[:], ALU.mult),
          reads=[p2_pbuf_b, p2_e_b], writes=[p2_w2_b])
        for h in range(2):
            o("tensor", (lambda h=h: lambda te: te.matmul(
                P7[:], p2_rep[:], p2_w2[:, h * 512:(h + 1) * 512],
                start=True, stop=True))(),
              reads=[p2_rep_b, p2_w2_b], writes=[P7_b])
            o("vector", (lambda h=h: lambda v: v.tensor_tensor(
                gkbc[:, h * 512:(h + 1) * 512], gkbc[:, h * 512:(h + 1) * 512],
                P7[:], ALU.add))(),
              reads=[P7_b], writes=[gkbc_b])

    def p1_batch(bi, rate, drip=None):
        base = bi * NTB
        stA(base)
        stA(base + 1)
        for s_ in range(NTB + 2):
            if s_ + 2 < NTB:
                stA(base + s_ + 2)
            if 2 <= s_ <= NTB + 1:
                stD(base + s_ - 2)
            if 1 <= s_ <= NTB:
                stC(base + s_ - 1)
            if s_ < NTB:
                stB(base + s_)
            drip_passA(rate)
            if drip is not None:
                drip()

    passB = ([("wd", k) for k in range(32)] + [("wu", k) for k in range(32)]
             + [("wg", k) for k in range(24)] + [("wg", k) for k in range(24, 32)])

    wf32a = wf32 + [wf32x_t[:, i * D:(i + 1) * D] for i in range(3)]
    wf32a_bl = list(wf32_bl) + list(wf32x_bl)

    def emit_passB(idx):
        nm, k = passB[idx]
        slot = idx % 2
        lslot = idx % 2 if idx < 40 else 2 + (idx % 3)
        if nm == "wd":
            src_ = wdT_h[k * 128:(k + 1) * 128, :]
            ridx = 10
            oap, tgt_b = wdst[slot], wdst_bl[slot]
        else:
            w_hh = wgT_h if nm == "wg" else wuT_h
            src_ = w_hh[(k // 4) * 128:(k // 4 + 1) * 128, (k % 4) * D:(k % 4 + 1) * D]
            ridx = 8 if nm == "wg" else 9
            tgt = wgT_sb if nm == "wg" else wuT_sb
            tgt_b = wgT_sb_b if nm == "wg" else wuT_sb_b
            oap = tgt[:, (k // 4) * I + (k % 4) * D:(k // 4) * I + (k % 4 + 1) * D]
        o(["sync", "scalar", "gpsimd"][idx % 3],
          (lambda src_=src_, lslot=lslot:
           lambda s: s.dma_start(out=wf32a[lslot], in_=src_))(),
          writes=[wf32a_bl[lslot]], dma=True)
        o("scalar", (lambda lslot=lslot, ridx=ridx: lambda s: s.activation(
            tstA[:], wf32a[lslot], AF.Sign, scale=wstat[:, ridx:ridx + 1],
            bias=bm5[:, 0:1]))(),
          reads=[wf32a_bl[lslot], wstat_b, bm5_b], writes=[tstA_b])
        o("scalar", (lambda lslot=lslot, ridx=ridx, oap=oap: lambda s: s.activation(
            oap, wf32a[lslot], AF.Sign, scale=wstat[:, ridx:ridx + 1],
            bias=bp5[:, 0:1]))(),
          reads=[wf32a_bl[lslot], wstat_b, bp5_b], writes=[tgt_b])
        o("vector", (lambda oap=oap: lambda v: v.tensor_tensor(
            oap, oap, tstA[:], ALU.add))(),
          reads=[tstA_b, tgt_b], writes=[tgt_b])
        if nm == "wd":
            o("gpsimd" if idx % 2 == 0 else "scalar",
              (lambda k=k, slot=slot: lambda s: s.dma_start(
                  out=swdT_h[k * 128:(k + 1) * 128, :], in_=wdst[slot]))(),
              reads=[tgt_b], writes=[swdT_b[k]], dma=True)

    pb_i = [0]

    def drip_passB(n, limit=None):
        lim = len(passB) if limit is None else limit
        for _ in range(n):
            if pb_i[0] < lim:
                emit_passB(pb_i[0])
                pb_i[0] += 1

    p1_batch(0, 2)
    for t in range(NTB):
        p2_tile(t)
    o("vector", lambda v: v.tensor_copy(gk_s[:, 0:D], gkbc[:]),
      reads=[gkbc_b], writes=[gk_s_b])
    o("vector", lambda v: v.memset(gqbc[:], 0.0), writes=[gqbc_b])
    o("vector", lambda v: v.memset(gkbc[:], 0.0), writes=[gkbc_b])
    p1_batch(1, 4)
    drip_passA(len(passA))

    for idx in range(3):
        o("vector", (lambda idx=idx: lambda v: v.reduce_sum(
            wstat[:, idx:idx + 1], wsum[:, idx * 32:(idx + 1) * 32], axis=AX.X))(),
          reads=[wsum_b], writes=[wstat_b])
    o("tensor", lambda te: te.matmul(P7[:, 0:16], ones_f[:], wstat[:, 0:16],
                                     start=True, stop=True),
      reads=[ones_b, wstat_b], writes=[P7_b])
    o("vector", lambda v: v.tensor_scalar_mul(wstat[:, 4:7], P7[:, 0:3], 1.0 / (I * D)),
      reads=[P7_b], writes=[wstat_b])
    o("vector", lambda v: v.tensor_scalar_max(wstat[:, 4:7], wstat[:, 4:7], 1e-5),
      reads=[wstat_b], writes=[wstat_b])
    o("vector", lambda v: v.reciprocal(wstat[:, 8:11], wstat[:, 4:7]),
      reads=[wstat_b], writes=[wstat_b])
    o("vector", lambda v: v.tensor_scalar_mul(wstat[:, 12:15], wstat[:, 4:7], 0.5),
      reads=[wstat_b], writes=[wstat_b])

    for t in range(NTB, NT):
        p2_tile(t)
        drip_passB(6, 88)
    o("vector", lambda v: v.tensor_copy(gk_s[:, D:2 * D], gkbc[:]),
      reads=[gkbc_b], writes=[gk_s_b])
    drip_passB(len(passB))

    gate_slots = [(PB[:, 0:512], PB_bl[0]), (PB[:, 512:1024], PB_bl[1])]
    up_slots = [(PA[:, 0:512], PA_bl[0]), (PA[:, 512:1024], PA_bl[1]),
                (PC[:, 0:512], PC_bl[0]), (PC[:, 512:1024], PC_bl[1])]

    def f1A(t):
        s2 = t % 2
        o("sync", (lambda t=t, s2=s2: lambda s: s.dma_start(
            out=f_xi[s2], in_=sxi_h[t * TP:(t + 1) * TP, :]))(),
          reads=[sxi_b[t]], writes=[f_xi_bl[s2]], dma=True)

    def f1_head(t):
        s2 = t % 2

        tr_pingpong(f_xi[s2], f_xi_bl[s2], f_xiT[s2], f_xiT_bl[s2])
        o("vector", (lambda t=t: lambda v: v.tensor_scalar(
            cgv[:, t:t + 1], xinvv[:, t:t + 1], wstat[:, 12:13], None, ALU.mult))(),
          reads=[xinvv_b, wstat_b], writes=[cgv_b])
        o("vector", (lambda t=t: lambda v: v.tensor_scalar(
            cuv[:, t:t + 1], xinvv[:, t:t + 1], wstat[:, 13:14], None, ALU.mult))(),
          reads=[xinvv_b, wstat_b], writes=[cuv_b])

    def f1_jj(t, jj):
        s2 = t % 2
        gps, gpb = gate_slots[jj % 2]
        ups, upb = up_slots[jj % 4]

        def mmjj(te, jj=jj, gps=gps, ups=ups, s2=s2):
            last = None
            for k in range(8):
                lhsT = f_xiT[s2][:, k * 128:(k + 1) * 128]
                last = te.matmul(gps, lhsT,
                                 wgT_sb[:, k * I + jj * 512:k * I + (jj + 1) * 512],
                                 start=(k == 0), stop=(k == 7))
                last = te.matmul(ups, lhsT,
                                 wuT_sb[:, k * I + jj * 512:k * I + (jj + 1) * 512],
                                 start=(k == 0), stop=(k == 7))
            return last
        o("tensor", mmjj, reads=[f_xiT_bl[s2], wgT_sb_b, wuT_sb_b],
          writes=[gpb, upb])
        o("scalar", (lambda t=t, jj=jj, gps=gps: lambda s: s.activation(
            f_g[jj % 2], gps, AF.Silu, scale=cgv[:, t:t + 1]))(),
          reads=[gpb, cgv_b], writes=[f_g_bl[jj % 2]])
        o("vector", (lambda t=t, jj=jj, ups=ups: lambda v: v.scalar_tensor_tensor(
            f_h[:, jj * 512:(jj + 1) * 512], ups, cuv[:, t:t + 1],
            f_g[jj % 2], ALU.mult, ALU.mult))(),
          reads=[upb, cuv_b, f_g_bl[jj % 2]], writes=[f_h_b])

    def f1_rowmax(t):
        slot = t % 4
        ss = slot * 8
        sb_ = stat_bl[slot]
        o("vector", (lambda ss=ss: lambda v: v.reduce_max(
            stat[:, ss + 7:ss + 8], f_h[:], axis=AX.X, apply_absolute_value=True))(),
          reads=[f_h_b], writes=[sb_])

    def f1C_head(t):
        slot = t % 4
        ss = slot * 8
        sb_ = stat_bl[slot]
        o("vector", (lambda ss=ss: lambda v: v.tensor_scalar_max(
            stat[:, ss + 7:ss + 8], stat[:, ss + 7:ss + 8], 1e-5))(),
          reads=[sb_], writes=[sb_])
        o("vector", (lambda t=t, ss=ss: lambda v: v.tensor_scalar_mul(
            hinvv[:, t:t + 1], stat[:, ss + 7:ss + 8], 1.0 / 127.0))(),
          reads=[sb_], writes=[hinvv_b])
        o("vector", (lambda t=t: lambda v: v.tensor_scalar(
            cdv[:, t:t + 1], hinvv[:, t:t + 1], wstat[:, 14:15], None, ALU.mult))(),
          reads=[hinvv_b, wstat_b], writes=[cdv_b])
        o("vector", (lambda ss=ss: lambda v: v.reciprocal(
            stat[:, ss + 7:ss + 8], stat[:, ss + 7:ss + 8]))(),
          reads=[sb_], writes=[sb_])
        o("vector", (lambda t=t, ss=ss: lambda v: v.tensor_scalar_mul(
            hsv[:, t:t + 1], stat[:, ss + 7:ss + 8], 127.0))(),
          reads=[sb_], writes=[hsv_b])

    def f1C_pair(t, p, jj_emit):
        qA, qB = 2 * p, 2 * p + 1
        hhA = f_h[:, qA * D:(qA + 1) * D]
        hhB = f_h[:, qB * D:(qB + 1) * D]
        o("vector", (lambda t=t, hhA=hhA: lambda v: v.tensor_scalar(
            fl1[:], hhA, hsv[:, t:t + 1], 128.5, ALU.mult, ALU.add))(),
          reads=[f_h_b, hsv_b], writes=[fl1_b])
        o("vector", (lambda t=t, hhB=hhB: lambda v: v.tensor_scalar(
            fl2[:], hhB, hsv[:, t:t + 1], 128.5, ALU.mult, ALU.add))(),
          reads=[f_h_b, hsv_b], writes=[fl2_b])
        for j in range(8):
            cc = float(2 ** (7 - j))
            scale = (2.0 ** 40) if j == 0 else -(2.0 ** 40)
            op1 = ALU.subtract if j == 0 else ALU.add
            o("scalar", (lambda j=j, scale=scale: lambda s: s.activation(
                rA[:], fl1[:], AF.Relu, scale=scale, bias=cbias[:, j:j + 1]))(),
              reads=[fl1_b, cbias_b], writes=[rA_b])
            o("vector", (lambda cc=cc, op1=op1: lambda v: v.scalar_tensor_tensor(
                fl1[:], rA[:], cc, fl1[:], ALU.min, op1))(),
              reads=[rA_b], writes=[fl1_b])
            o("scalar", (lambda j=j, scale=scale: lambda s: s.activation(
                rB[:], fl2[:], AF.Relu, scale=scale, bias=cbias[:, j:j + 1]))(),
              reads=[fl2_b, cbias_b], writes=[rB_b])
            o("vector", (lambda cc=cc, op1=op1: lambda v: v.scalar_tensor_tensor(
                fl2[:], rB[:], cc, fl2[:], ALU.min, op1))(),
              reads=[rB_b], writes=[fl2_b])
        o("vector", (lambda t=t, qA=qA, hhA=hhA: lambda v: v.scalar_tensor_tensor(
            f_hq[:, qA * D:(qA + 1) * D], hhA, hsv[:, t:t + 1], fl1[:],
            ALU.mult, ALU.add))(),
          reads=[f_h_b, hsv_b, fl1_b], writes=[f_hq_b])
        o("vector", (lambda t=t, qB=qB, hhB=hhB: lambda v: v.scalar_tensor_tensor(
            f_hq[:, qB * D:(qB + 1) * D], hhB, hsv[:, t:t + 1], fl2[:],
            ALU.mult, ALU.add))(),
          reads=[f_h_b, hsv_b, fl2_b], writes=[f_hq_b])
        jj_emit()

    def f1C_dma(t):
        o("sync", (lambda t=t: lambda s: s.dma_start(
            out=shqT_h[t * TP:(t + 1) * TP, :], in_=f_hq[:]))(),
          reads=[f_hq_b], writes=[shqT_b[t]], dma=True)

    f1A(0)
    f1A(1)
    f1_head(0)
    for jj in range(8):
        f1_jj(0, jj)
    f1_rowmax(0)
    for s_ in range(1, NT + 1):
        if s_ + 1 < NT:
            f1A(s_ + 1)
        if s_ < NT:
            f1_head(s_)
        f1C_head(s_ - 1)
        for p_ in range(2):
            def jj_emit(s_=s_, p_=p_):
                if s_ < NT:
                    for j4 in range(4):
                        f1_jj(s_, 4 * p_ + j4)
            f1C_pair(s_ - 1, p_, jj_emit)
        f1C_dma(s_ - 1)
        if s_ < NT:
            f1_rowmax(s_)

    for k in range(8):
        o("gpsimd", (lambda k=k: lambda g: g.dma_start(
            out=wr_sb[:, k * D:(k + 1) * D], in_=wr_h[k * 128:(k + 1) * 128, :]))(),
          writes=[wr_sb_b], dma=True)
    for k in range(32):
        o("sync", (lambda k=k: lambda s: s.dma_start(
            out=wdT_sb[:, k * D:(k + 1) * D], in_=swdT_h[k * 128:(k + 1) * 128, :]))(),
          reads=[swdT_b[k]], writes=[wdT_sb_b], dma=True)
    for h2 in range(2):
        def rsmm(te, h2=h2):
            last = None
            for k in range(32):
                last = te.matmul(P7[0:1, :], halfv[:],
                                 wdT_sb[:, k * D + h2 * 512:k * D + (h2 + 1) * 512],
                                 start=(k == 0), stop=(k == 31))
            return last
        o("tensor", rsmm, reads=[halfv_b, wdT_sb_b], writes=[P7_b])
        o("vector", (lambda h2=h2: lambda v: v.tensor_copy(
            nrs_sb[:, h2 * 512:(h2 + 1) * 512], P7[0:1, :]))(),
          reads=[P7_b], writes=[nrs_sb_b])
    for h2 in range(2):
        o("tensor", (lambda h2=h2: lambda te: te.matmul(
            P7[:], ones_f[0:1, :], nrs_sb[:, h2 * 512:(h2 + 1) * 512],
            start=True, stop=True))(),
          reads=[ones_b, nrs_sb_b], writes=[P7_b])
        o("vector", (lambda h2=h2: lambda v: v.tensor_copy(
            corr_bc[:, h2 * 512:(h2 + 1) * 512], P7[:]))(),
          reads=[P7_b], writes=[corr_bc_b])

    def f2A(t):
        s2 = t % 2
        o("sync", (lambda t=t, s2=s2: lambda s: s.dma_start(
            out=z_hq[s2], in_=shqT_h[t * TP:(t + 1) * TP, :]))(),
          reads=[shqT_b[t]], writes=[z_hq_bl[s2]], dma=True)
        o("scalar", (lambda t=t, s2=s2: lambda s: s.dma_start(
            out=z_v[s2], in_=sV_h[t * TP:(t + 1) * TP, :]))(),
          reads=[sV_b[t]], writes=[z_v_bl[s2]], dma=True)
        o("scalar", (lambda t=t: lambda s: s.dma_start(
            out=z_acc[t % 4], in_=sacc_h[t * TP:(t + 1) * TP, :]))(),
          reads=[sacc_b[t]], writes=[z_acc_bl[t % 4]], dma=True)

    def f2B(t):
        s2 = t % 2
        bi = t // NTB
        o("vector", (lambda bi=bi, s2=s2: lambda v: v.tensor_tensor(
            z_u[:], gk_s[:, bi * D:(bi + 1) * D], z_v[s2], ALU.mult))(),
          reads=[gk_s_b, z_v_bl[s2]], writes=[z_u_b])

        tr_pingpong(z_u, z_u_b, z_uT[s2], z_uT_bl[s2])
        for r in range(4):
            tr_pingpong(z_hq[s2], z_hq_bl[s2],
                        z_hqT[s2][:, r * 1024:(r + 1) * 1024], z_hqT_bl[s2],
                        src_off=r * 1024)

    def f2C(t):
        s2 = t % 2

        def rmm(te, s2=s2):
            last = None
            for k in range(8):
                lhsT = z_uT[s2][:, k * 128:(k + 1) * 128]
                for h in range(2):
                    last = te.matmul(PB[:, h * 512:(h + 1) * 512], lhsT,
                                     wr_sb[:, k * D + h * 512:k * D + (h + 1) * 512],
                                     start=(k == 0), stop=(k == 7))
            return last
        o("tensor", rmm, reads=[z_uT_bl[s2], wr_sb_b], writes=PB_bl)

        dps = PA if (t % 2 == 0) else PC
        dpb = PA_bl if (t % 2 == 0) else PC_bl

        def dmm(te, dps=dps, s2=s2):
            last = None
            for k in range(32):
                lhsT = z_hqT[s2][:, k * 128:(k + 1) * 128]
                for h in range(2):
                    last = te.matmul(dps[:, h * 512:(h + 1) * 512], lhsT,
                                     wdT_sb[:, k * D + h * 512:k * D + (h + 1) * 512],
                                     start=(k == 0), stop=(k == 31))
            return last
        o("tensor", dmm, reads=[z_hqT_bl[s2], wdT_sb_b], writes=dpb)

    def f2D(t):
        s2 = t % 2
        dps = PA if (t % 2 == 0) else PC
        dpb = PA_bl if (t % 2 == 0) else PC_bl
        o("vector", (lambda t=t: lambda v: v.tensor_tensor(
            z_a2[:], z_acc[t % 4], PB[:], ALU.add))(),
          reads=[z_acc_bl[t % 4]] + PB_bl, writes=[z_a2_b])
        o("vector", (lambda t=t: lambda v: v.scalar_tensor_tensor(
            z_a2[:], corr_bc[:], cdv[:, t:t + 1], z_a2[:], ALU.mult, ALU.add))(),
          reads=[corr_bc_b, cdv_b, z_a2_b], writes=[z_a2_b])
        for h in range(2):
            o("vector", (lambda t=t, h=h, dps=dps, s2=s2: lambda v: v.scalar_tensor_tensor(
                z_y[s2][:, h * 512:(h + 1) * 512], dps[:, h * 512:(h + 1) * 512],
                cdv[:, t:t + 1], z_a2[:, h * 512:(h + 1) * 512], ALU.mult, ALU.add))(),
              reads=[dpb[h], cdv_b, z_a2_b], writes=[z_y_bl[s2]])
        o("sync", (lambda t=t, s2=s2: lambda s: s.dma_start(
            out=y_h[t * TP:(t + 1) * TP, :], in_=z_y[s2]))(),
          reads=[z_y_bl[s2]], writes=[], dma=True)

    f2A(0)
    f2A(1)
    for s_ in range(NT + 2):
        if s_ < NT:
            f2B(s_)
        if s_ >= 2:
            f2D(s_ - 2)
        if 1 <= s_ <= NT:
            f2C(s_ - 1)
        if s_ + 2 < NT:
            f2A(s_ + 2)

    from contextlib import ExitStack
    with ExitStack() as st:
        semh = {s: st.enter_context(nc.semaphore("sem_" + s)) for s in ALL_SEMS}
        block = st.enter_context(nc.Block())

        def mk(engname):
            def _f(eng):
                for waits, build, sem, amt in pr.items[engname]:
                    for s_, v_ in sorted(waits.items()):
                        eng.wait_ge(semh[s_], v_)
                    inst = build(eng)
                    inst.then_inc(semh[sem], amt)
                if engname == "sync":
                    for ln in DMA_LANES["DS"]:
                        if pr.counts[ln]:
                            eng.wait_ge(semh[ln], pr.counts[ln])
            return _f

        block.sync(mk("sync"))
        block.gpsimd(mk("gpsimd"))
        block.vector(mk("vector"))
        block.scalar(mk("scalar"))
        block.tensor(mk("tensor"))

    return nc


_NC_CACHE = []


def kernel(L0_attn_x, L0_attn_Wq, L0_attn_Wk, L0_attn_Wv, L0_attn_Wr,
           L0_ffn_w_gate, L0_ffn_w_up, L0_ffn_w_down):
    x = np.ascontiguousarray(np.asarray(L0_attn_x, dtype=np.float32))
    wq = np.ascontiguousarray(np.asarray(L0_attn_Wq, dtype=np.float32))
    wk = np.ascontiguousarray(np.asarray(L0_attn_Wk, dtype=np.float32))
    wv = np.ascontiguousarray(np.asarray(L0_attn_Wv, dtype=np.float32))
    wr = np.ascontiguousarray(np.asarray(L0_attn_Wr, dtype=np.float32))
    wgT = np.ascontiguousarray(np.asarray(L0_ffn_w_gate, dtype=np.float32).T)
    wuT = np.ascontiguousarray(np.asarray(L0_ffn_w_up, dtype=np.float32).T)
    wdT = np.ascontiguousarray(np.asarray(L0_ffn_w_down, dtype=np.float32).T)
    if not _NC_CACHE:
        _NC_CACHE.append(build_nc())
    nc = _NC_CACHE[0]
    in_maps = []
    for c in range(8):
        in_maps.append({
            "x": np.ascontiguousarray(x[2 * c:2 * c + 2].reshape(NT * TP, D)),
            "wq": wq, "wk": wk, "wv": wv, "wr": wr,
            "wgT": wgT, "wuT": wuT, "wdT": wdT,
        })
    res = run_bass_kernel_spmd(nc, in_maps, core_ids=list(range(8)))
    out = np.concatenate([r["y"].reshape(B, S, D) for r in res.results], axis=0)
    return out


# version 26
# speedup vs baseline: 1.0497x; 1.0113x over previous
import numpy as np
import concourse.bass as bass
import concourse.mybir as mybir
from concourse.bass_utils import run_bass_kernel_spmd

F32 = mybir.dt.float32
BF16 = mybir.dt.bfloat16
AF = mybir.ActivationFunctionType
ALU = mybir.AluOpType
AX = mybir.AxisListType

B = 2
S = 2048
D = 1024
I = 4096
TP = 128
NTB = S // TP
NT = B * NTB
SCALE = float(D) ** -0.5
EPS = 1e-5
KB = 1024

ENG_SEM = {"tensor": "T", "vector": "V", "scalar": "S", "gpsimd": "G"}
DMA_SEM = {"sync": "DS", "scalar": "DC", "gpsimd": "DG"}
NLANES = 6
DMA_LANES = {q: ["%s%d" % (q, i) for i in range(NLANES)] for q in ["DS", "DC", "DG"]}
ALL_SEMS = (["T", "V", "S", "G"]
            + DMA_LANES["DS"] + DMA_LANES["DC"] + DMA_LANES["DG"])


class Buf:
    __slots__ = ("writer", "readers", "aliases", "name")

    def __init__(self, name=""):
        self.writer = None
        self.readers = {}
        self.aliases = []
        self.name = name


class Prog:
    def __init__(self):
        self.counts = {s: 0 for s in ALL_SEMS}
        self.items = {e: [] for e in ["tensor", "vector", "scalar", "gpsimd", "sync"]}
        self.lastw = {}
        self.rr = {q: 0 for q in ["DS", "DC", "DG"]}

    def op(self, eng, build, reads=(), writes=(), dma=False):
        if dma:
            q = DMA_SEM[eng]
            sem = DMA_LANES[q][self.rr[q] % NLANES]
            self.rr[q] += 1
        else:
            sem = ENG_SEM[eng]
        amt = 16 if dma else 1
        waits = {}

        def need(mark):
            if not mark:
                return
            s, v = mark
            if s == "T" and eng == "tensor" and not dma:
                return
            if self.lastw.get((eng, s), 0) >= v:
                return
            if waits.get(s, 0) < v:
                waits[s] = v

        def need_all(b):
            need(b.writer)
            for s, v in b.readers.items():
                need((s, v))

        if dma and self.counts[sem] > 0:
            need((sem, self.counts[sem]))
        for b in reads:
            need(b.writer)
        for b in writes:
            need_all(b)
            for ab in b.aliases:
                need_all(ab)
        self.counts[sem] += amt
        mark = (sem, self.counts[sem])
        for s, v in waits.items():
            if self.lastw.get((eng, s), 0) < v:
                self.lastw[(eng, s)] = v
        self.items[eng].append((waits, build, sem, amt))
        for b in reads:
            if b.readers.get(sem, 0) < mark[1]:
                b.readers[sem] = mark[1]
        for b in writes:
            b.writer = mark
            b.readers = {}
        return mark


def _bpp(shape, dtype):
    n = 1
    for d in shape[1:]:
        n *= d
    return n * mybir.dt.size(dtype)


def build_nc():
    nc = bass.Bass("TRN2", target_bir_lowering=False)
    pr = Prog()
    o = pr.op

    x_h = nc.dram_tensor("x", [NT * TP, D], F32, kind="ExternalInput").ap()
    wq_h = nc.dram_tensor("wq", [D, D], F32, kind="ExternalInput").ap()
    wk_h = nc.dram_tensor("wk", [D, D], F32, kind="ExternalInput").ap()
    wv_h = nc.dram_tensor("wv", [D, D], F32, kind="ExternalInput").ap()
    wr_h = nc.dram_tensor("wr", [D, D], F32, kind="ExternalInput").ap()
    wgT_h = nc.dram_tensor("wgT", [D, I], F32, kind="ExternalInput").ap()
    wuT_h = nc.dram_tensor("wuT", [D, I], F32, kind="ExternalInput").ap()
    wdT_h = nc.dram_tensor("wdT", [I, D], F32, kind="ExternalInput").ap()
    y_h = nc.dram_tensor("y", [NT * TP, D], F32, kind="ExternalOutput").ap()
    sacc_h = nc.dram_tensor("sacc", [NT * TP, D], F32, kind="Internal").ap()
    sxi_h = nc.dram_tensor("sxi", [NT * TP, D], BF16, kind="Internal").ap()
    sV_h = nc.dram_tensor("sV", [NT * TP, D], BF16, kind="Internal").ap()
    shqT_h = nc.dram_tensor("shqT", [NT * TP, I], BF16, kind="Internal").ap()
    swdT_h = nc.dram_tensor("swdT", [I, D], BF16, kind="Internal").ap()

    sacc_b = [Buf("sacc%d" % t) for t in range(NT)]
    sxi_b = [Buf("sxi%d" % t) for t in range(NT)]
    sV_b = [Buf("sV%d" % t) for t in range(NT)]
    shqT_b = [Buf("shqT%d" % t) for t in range(NT)]
    swdT_b = [Buf("swdT%d" % k) for k in range(32)]

    allocs = []

    def sb(name, shape, dtype, off_kb, nbuf=1):
        off = int(off_kb * KB)
        h = nc.alloc_sbuf_tensor_at(name, shape, dtype, offset=off)
        end = off + _bpp(shape, dtype)
        assert end <= int(207.8 * KB), (name, end)
        bufs = Buf(name) if nbuf == 1 else [Buf("%s%d" % (name, i)) for i in range(nbuf)]
        bl = bufs if isinstance(bufs, list) else [bufs]
        for (s0, e0, bl0) in allocs:
            if s0 < end and off < e0:
                for bb in bl:
                    for ob in bl0:
                        bb.aliases.append(ob)
                        ob.aliases.append(bb)
        allocs.append((off, end, bl))
        return h.ap(), bufs

    ident, ident_b = sb("ident", [128, 128], BF16, 18.0)
    ones_f, ones_b = sb("ones_f", [128, 128], F32, 18.25)
    svec, svec_b = sb("svec", [128, NT], F32, 18.75)
    xinvv, xinvv_b = sb("xinvv", [128, NT], F32, 18.875)
    hinvv, hinvv_b = sb("hinvv", [128, NT], F32, 19.0)
    cgv, cgv_b = sb("cgv", [128, NT], F32, 19.125)
    cuv, cuv_b = sb("cuv", [128, NT], F32, 19.25)
    cdv, cdv_b = sb("cdv", [128, NT], F32, 19.375)
    hsv, hsv_b = sb("hsv", [128, NT], F32, 19.5)
    epsv, epsv_b = sb("epsv", [128, 1], F32, 19.625)
    cbias, cbias_b = sb("cbias", [128, 8], F32, 19.75)
    halfv, halfv_b = sb("halfv", [128, 1], BF16, 19.8125)
    bm5, bm5_b = sb("bm5", [128, 1], F32, 19.875)
    bp5, bp5_b = sb("bp5", [128, 1], F32, 19.9375)
    wsum, wsum_b = sb("wsum", [128, 96], F32, 20.0)
    wstat, wstat_b = sb("wstat", [128, 16], F32, 20.375)
    stat, stat_bl = sb("stat", [128, 32], F32, 20.4375, nbuf=4)
    gk_s, gk_s_b = sb("gk_s", [128, 2 * D], F32, 20.625)
    gqbc, gqbc_b = sb("gqbc", [128, D], F32, 28.625)
    gkbc, gkbc_b = sb("gkbc", [128, D], F32, 32.625)
    iotat, iotat_b = sb("iotat", [128, 128], F32, 37.125)
    Kst, Kst_bl = sb("Kst", [128, NTB * D], BF16, 37.125, nbuf=NTB)
    wq_sb, wq_sb_b = sb("wq_sb", [128, 8 * D], BF16, 69.125)
    wk_sb, wk_sb_b = sb("wk_sb", [128, 8 * D], BF16, 85.125)
    wv_sb, wv_sb_b = sb("wv_sb", [128, 8 * D], BF16, 101.125)
    NXB = 6
    xbuf_t, xbuf_bl4 = sb("xbuf", [128, 4 * D], F32, 117.125, nbuf=4)
    xbuf2_t, xbuf_bl2 = sb("xbuf2", [128, 2 * D], F32, 171.375, nbuf=2)
    xbuf = ([xbuf_t[:, i * D:(i + 1) * D] for i in range(4)]
            + [xbuf2_t[:, i * D:(i + 1) * D] for i in range(2)])
    xbuf_bl = xbuf_bl4 + xbuf_bl2
    xnf_t, xnf_bl = sb("xnf", [128, D], F32, 133.125, nbuf=1)
    xnf2_t, xnf2_bl = sb("xnf2", [128, D], F32, 179.375, nbuf=1)
    xnf_s = [xnf_t, xnf2_t]
    xnf_sbl = [xnf_bl, xnf2_bl]
    sqs, sqs_b = sb("sqs", [128, D], F32, 137.125)
    xi16b_t, xi16b_bl = sb("xi16b", [128, D], BF16, 183.375)
    accs2_t, accs2_bl = sb("accs2", [128, D], F32, 185.375)
    v16b_t, v16b_bl = sb("v16b", [128, D], BF16, 189.375)
    e_sb, e_sb_b = sb("e_sb", [128, D], F32, 141.125)
    accs, accs_b = sb("accs", [128, D], F32, 145.125)
    lh1, lh1_b = sb("lh1", [128, D], F32, 149.125)
    lh2, lh2_b = sb("lh2", [128, D], F32, 153.125)
    xn16_t, xn16_bl = sb("xn16", [128, 2 * D], BF16, 157.125, nbuf=2)
    xn16 = [xn16_t[:, i * D:(i + 1) * D] for i in range(2)]
    xnT_t, xnT_bl = sb("xnT_s", [128, 2 * D], BF16, 161.125, nbuf=2)
    xnT_s = [xnT_t[:, i * D:(i + 1) * D] for i in range(2)]
    xi16, xi16_b = sb("xi16", [128, D], BF16, 165.125)
    v16, v16_b = sb("v16", [128, D], BF16, 167.125)
    w2_s, w2_s_b = sb("w2_s", [128, D], BF16, 169.125)
    rep_s, rep_s_b = sb("rep_s", [128, 128], BF16, 171.125)
    wf32_t, wf32_bl = sb("wf32", [128, 2 * D], F32, 197.375, nbuf=2)
    wf32 = [wf32_t[:, i * D:(i + 1) * D] for i in range(2)]
    wabs, wabs_b = sb("wabs", [128, D], BF16, 205.375)
    p2_pbuf, p2_pbuf_b = sb("p2_pbuf", [128, D], F32, 117.15625)
    p2_e, p2_e_b = sb("p2_e", [128, D], F32, 121.15625)
    p2_w2, p2_w2_b = sb("p2_w2", [128, D], BF16, 125.15625)
    p2_rep, p2_rep_b = sb("p2_rep", [128, 128], BF16, 127.15625)
    tstA, tstA_b = sb("tstA", [128, D], BF16, 205.375)
    wdst_t, wdst_bl = sb("wdst", [128, 2 * D], BF16, 69.125, nbuf=2)
    wdst = [wdst_t[:, i * D:(i + 1) * D] for i in range(2)]
    wf32x_t, wf32x_bl = sb("wf32x", [128, 3 * D], F32, 37.125, nbuf=3)
    wgT_sb, wgT_sb_b = sb("wgT_sb", [128, 8 * I], BF16, 69.125)
    wuT_sb, wuT_sb_b = sb("wuT_sb", [128, 8 * I], BF16, 133.125)
    f_xi_t, f_xi_bl = sb("f_xi", [128, 2 * D], BF16, 37.125, nbuf=2)
    f_xi = [f_xi_t[:, i * D:(i + 1) * D] for i in range(2)]
    f_xiT_t, f_xiT_bl = sb("f_xiT", [128, 2 * D], BF16, 41.125, nbuf=2)
    f_xiT = [f_xiT_t[:, i * D:(i + 1) * D] for i in range(2)]
    f_g_t, f_g_bl = sb("f_g", [128, 2 * 512], F32, 45.125, nbuf=2)
    f_g = [f_g_t[:, i * 512:(i + 1) * 512] for i in range(2)]
    f_h, f_h_b = sb("f_h", [128, I], F32, 49.125)
    fl1, fl1_b = sb("fl1", [128, D], F32, 28.625)
    fl2, fl2_b = sb("fl2", [128, D], F32, 32.625)
    rA, rA_b = sb("rA", [128, D], BF16, 65.125)
    rB, rB_b = sb("rB", [128, D], BF16, 67.125)
    f_hq, f_hq_b = sb("f_hq", [128, I], BF16, 197.375)
    wr_sb, wr_sb_b = sb("wr_sb", [128, 8 * D], BF16, 37.125)
    wdT_sb, wdT_sb_b = sb("wdT_sb", [128, 32 * D], BF16, 133.125)
    z_v_t, z_v_bl = sb("z_v", [128, 2 * D], BF16, 53.125, nbuf=2)
    z_v = [z_v_t[:, i * D:(i + 1) * D] for i in range(2)]
    z_u, z_u_b = sb("z_u", [128, D], BF16, 57.125)
    z_uT_t, z_uT_bl = sb("z_uT", [128, 2 * D], BF16, 59.125, nbuf=2)
    z_uT = [z_uT_t[:, i * D:(i + 1) * D] for i in range(2)]
    z_a2, z_a2_b = sb("z_a2", [128, D], F32, 63.125)
    nrs_sb, nrs_sb_b = sb("nrs_sb", [1, D], F32, 125.125)
    z_hq_t, z_hq_bl = sb("z_hq", [128, 2 * I], BF16, 69.125, nbuf=2)
    z_hq = [z_hq_t[:, i * I:(i + 1) * I] for i in range(2)]
    z_hqT_t, z_hqT_bl = sb("z_hqT", [128, 2 * I], BF16, 85.125, nbuf=2)
    z_hqT = [z_hqT_t[:, i * I:(i + 1) * I] for i in range(2)]
    z_acc_t, z_acc_bl = sb("z_acc", [128, 4 * D], F32, 101.125, nbuf=4)
    z_acc = [z_acc_t[:, i * D:(i + 1) * D] for i in range(4)]
    z_y_t, z_y_bl = sb("z_y", [128, 2 * D], F32, 117.125, nbuf=2)
    z_y = [z_y_t[:, i * D:(i + 1) * D] for i in range(2)]
    corr_bc, corr_bc_b = sb("corr_bc", [128, D], F32, 28.625)

    PA_h = nc.alloc_psum_tensor("PA", [128, D], F32)
    PB_h = nc.alloc_psum_tensor("PB", [128, D], F32)
    PC_h = nc.alloc_psum_tensor("PC", [128, D], F32)
    PT_h = nc.alloc_psum_tensor("PT", [128, 2 * D], BF16)
    PA, PB, PC, PT = PA_h.ap(), PB_h.ap(), PC_h.ap(), PT_h.ap()
    P7 = PB[:, 0:512]
    PTf = PT.bitcast(F32)
    PA_bl = [Buf("PA0"), Buf("PA1")]
    PB_bl = [Buf("PB0"), Buf("PB1")]
    PC_bl = [Buf("PC0"), Buf("PC1")]
    PT_b = Buf("PT")
    PTX = [PT[:, 0:D], PT[:, D:2 * D]]
    P7_b = Buf("P7")
    PTX_bl = [Buf("PTX0"), Buf("PTX1")]
    P7_b.aliases.append(PB_bl[0])
    PB_bl[0].aliases.append(P7_b)
    trrot = [0]

    def tr_pingpong(src_ap, src_b, dst_ap, dst_b, src_off=0):
        hb = trrot[0] % 2
        trrot[0] += 1
        buf_ap, buf_b = PTX[hb], PTX_bl[hb]

        def fill(te, buf_ap=buf_ap, src_ap=src_ap, src_off=src_off):
            last = None
            for k in range(8):
                last = te.transpose(buf_ap[:, k * 128:(k + 1) * 128],
                                    src_ap[:, src_off + k * 128:src_off + (k + 1) * 128],
                                    ident[:])
            return last
        o("tensor", fill, reads=[src_b, ident_b], writes=[buf_b])
        o("scalar", (lambda buf_ap=buf_ap, dst_ap=dst_ap:
                     lambda s: s.copy(dst_ap[:, 0:D], buf_ap[:, 0:D]))(),
          reads=[buf_b], writes=[dst_b])

    o("gpsimd", lambda g: g.iota(iotat[:], [[1, 128]], channel_multiplier=-1,
                                 allow_small_or_imprecise_dtypes=True),
      writes=[iotat_b])
    o("vector", lambda v: v.tensor_single_scalar(ident[:], iotat[:], 0.0, ALU.is_equal),
      reads=[iotat_b], writes=[ident_b])
    o("vector", lambda v: v.memset(ones_f[:], 1.0), writes=[ones_b])
    o("vector", lambda v: v.memset(gqbc[:], 0.0), writes=[gqbc_b])
    o("vector", lambda v: v.memset(gkbc[:], 0.0), writes=[gkbc_b])
    o("vector", lambda v: v.memset(wsum[:], 0.0), writes=[wsum_b])
    o("vector", lambda v: v.memset(wstat[:], 0.0), writes=[wstat_b])
    o("vector", lambda v: v.memset(epsv[:], EPS), writes=[epsv_b])
    o("vector", lambda v: v.memset(halfv[:], 0.5), writes=[halfv_b])
    o("vector", lambda v: v.memset(bm5[:], -0.5), writes=[bm5_b])
    o("vector", lambda v: v.memset(bp5[:], 0.5), writes=[bp5_b])
    for j in range(8):
        cval = -float(2 ** (7 - j)) * (2.0 ** 40)
        o("vector", (lambda j=j, cval=cval: lambda v: v.memset(
            cbias[:, j:j + 1], cval))(), writes=[cbias_b])

    for w_hh, w_sbb, w_bb in [(wq_h, wq_sb, wq_sb_b), (wk_h, wk_sb, wk_sb_b),
                              (wv_h, wv_sb, wv_sb_b)]:
        for k in range(8):
            o("gpsimd",
              (lambda w_hh=w_hh, w_sbb=w_sbb, k=k:
               lambda g: g.dma_start(out=w_sbb[:, k * D:(k + 1) * D],
                                     in_=w_hh[k * 128:(k + 1) * 128, :]))(),
              writes=[w_bb], dma=True)

    passA = ([("wg", k) for k in range(32)] + [("wu", k) for k in range(32)]
             + [("wd", k) for k in range(32)])

    def emit_passA(idx):
        nm, k = passA[idx]
        slot = idx % 2
        if nm == "wd":
            src = wdT_h[k * 128:(k + 1) * 128, :]
        else:
            w_hh = wgT_h if nm == "wg" else wuT_h
            src = w_hh[(k // 4) * 128:(k // 4 + 1) * 128, (k % 4) * D:(k % 4 + 1) * D]
        col = {"wg": 0, "wu": 32, "wd": 64}[nm] + k
        o("scalar", (lambda src=src, slot=slot:
                     lambda s: s.dma_start(out=wf32[slot], in_=src))(),
          writes=[wf32_bl[slot]], dma=True)
        o("scalar", (lambda slot=slot, col=col:
                     lambda s: s.activation(wabs[:], wf32[slot], AF.Abs,
                                            accum_out=wsum[:, col:col + 1]))(),
          reads=[wf32_bl[slot]], writes=[wabs_b, wsum_b])

    pa_i = [0]

    def drip_passA(n):
        for _ in range(n):
            if pa_i[0] < len(passA):
                emit_passA(pa_i[0])
                pa_i[0] += 1

    def stA(t):
        slot = t % NXB
        o("sync", (lambda t=t, slot=slot:
                   lambda s: s.dma_start(out=xbuf[slot], in_=x_h[t * TP:(t + 1) * TP, :]))(),
          writes=[xbuf_bl[slot]], dma=True)

    def stB(t):
        slot = t % 4
        ss = slot * 8
        sb_ = stat_bl[slot]
        xb = xbuf[t % NXB]
        xbb = xbuf_bl[t % NXB]
        c = lambda i: stat[:, ss + i:ss + i + 1]
        o("vector", (lambda xb=xb, ss=ss: lambda v: v.reduce_sum(
            stat[:, ss:ss + 1], xb, axis=AX.X))(), reads=[xbb], writes=[sb_])
        o("vector", (lambda ss=ss: lambda v: v.tensor_scalar_mul(
            stat[:, ss + 1:ss + 2], stat[:, ss:ss + 1], 1.0 / D))(),
          reads=[sb_], writes=[sb_])
        o("scalar", (lambda xb=xb, ss=ss: lambda s: s.activation(
            sqs[:], xb, AF.Square, accum_out=stat[:, ss + 2:ss + 3]))(),
          reads=[xbb], writes=[sqs_b, sb_])
        o("vector", (lambda ss=ss: lambda v: v.tensor_scalar_mul(
            stat[:, ss + 2:ss + 3], stat[:, ss + 2:ss + 3], 1.0 / D))(),
          reads=[sb_], writes=[sb_])
        o("vector", (lambda ss=ss: lambda v: v.scalar_tensor_tensor(
            stat[:, ss + 3:ss + 4], stat[:, ss + 1:ss + 2], stat[:, ss + 1:ss + 2],
            stat[:, ss + 2:ss + 3], ALU.mult, ALU.subtract))(),
          reads=[sb_], writes=[sb_])
        o("scalar", (lambda ss=ss: lambda s: s.activation(
            stat[:, ss + 4:ss + 5], stat[:, ss + 3:ss + 4], AF.Ln,
            scale=-1.0, bias=epsv[:, 0:1]))(), reads=[sb_, epsv_b], writes=[sb_])
        o("scalar", (lambda ss=ss: lambda s: s.activation(
            stat[:, ss + 5:ss + 6], stat[:, ss + 4:ss + 5], AF.Exp,
            scale=-0.5))(), reads=[sb_], writes=[sb_])
        xnf = xnf_s[t % 2]
        xnf_b = xnf_sbl[t % 2]
        xi16_, xi16_b_ = (xi16, xi16_b) if t % 2 == 0 else (xi16b_t, xi16b_bl)
        o("vector", (lambda xb=xb, ss=ss, xnf=xnf: lambda v: v.tensor_scalar(
            xnf[:], xb, stat[:, ss + 1:ss + 2], stat[:, ss + 5:ss + 6],
            ALU.subtract, ALU.mult))(),
          reads=[xbb, sb_], writes=[xnf_b])
        o("vector", (lambda ss=ss, xnf=xnf: lambda v: v.reduce_max(
            stat[:, ss + 7:ss + 8], xnf[:], axis=AX.X, apply_absolute_value=True))(),
          reads=[xnf_b], writes=[sb_])
        o("vector", (lambda ss=ss: lambda v: v.tensor_scalar_max(
            stat[:, ss + 7:ss + 8], stat[:, ss + 7:ss + 8], 1e-5))(),
          reads=[sb_], writes=[sb_])
        o("vector", (lambda t=t, ss=ss: lambda v: v.tensor_scalar_mul(
            xinvv[:, t:t + 1], stat[:, ss + 7:ss + 8], 1.0 / 127.0))(),
          reads=[sb_], writes=[xinvv_b])
        o("vector", (lambda ss=ss: lambda v: v.reciprocal(
            stat[:, ss + 7:ss + 8], stat[:, ss + 7:ss + 8]))(),
          reads=[sb_], writes=[sb_])
        o("vector", (lambda t=t, ss=ss: lambda v: v.tensor_scalar_mul(
            svec[:, t:t + 1], stat[:, ss + 7:ss + 8], 127.0))(),
          reads=[sb_], writes=[svec_b])
        o("scalar", (lambda t=t, xnf=xnf: lambda s: s.copy(xn16[t % 2], xnf[:]))(),
          reads=[xnf_b], writes=[xn16_bl[t % 2]])
        o("vector", (lambda t=t, xnf=xnf: lambda v: v.tensor_scalar(
            lh1[:], xnf[:], svec[:, t:t + 1], 128.5, ALU.mult, ALU.add))(),
          reads=[xnf_b, svec_b], writes=[lh1_b])
        for kb in range(7, -1, -1):
            cc = float(2 ** kb)
            o("vector", (lambda cc=cc: lambda v: v.tensor_scalar(
                lh2[:], lh1[:], cc, cc, ALU.is_ge, ALU.mult))(),
              reads=[lh1_b], writes=[lh2_b])
            o("vector", lambda v: v.tensor_tensor(lh1[:], lh1[:], lh2[:], ALU.subtract),
              reads=[lh2_b], writes=[lh1_b])
        o("vector", (lambda t=t, xnf=xnf: lambda v: v.tensor_scalar(
            lh2[:], xnf[:], svec[:, t:t + 1], 0.5, ALU.mult, ALU.add))(),
          reads=[xnf_b, svec_b], writes=[lh2_b])
        o("vector", (lambda xi16_=xi16_: lambda v: v.tensor_tensor(
            xi16_[:], lh2[:], lh1[:], ALU.subtract))(),
          reads=[lh2_b, lh1_b], writes=[xi16_b_])
        o("gpsimd", (lambda t=t, xi16_=xi16_: lambda g: g.dma_start(
            out=sxi_h[t * TP:(t + 1) * TP, :], in_=xi16_[:]))(),
          reads=[xi16_b_], writes=[sxi_b[t]], dma=True)

    def stC(t):
        s2 = t % 2

        tr_pingpong(xn16[s2], xn16_bl[s2], xnT_s[s2], xnT_bl[s2])

        def qkv(te):
            last = None
            for k in range(8):
                lhsT = xnT_s[s2][:, k * 128:(k + 1) * 128]
                for (ws, ps) in [(wq_sb, PA), (wk_sb, PB), (wv_sb, PC)]:
                    for h in range(2):
                        last = te.matmul(ps[:, h * 512:(h + 1) * 512], lhsT,
                                         ws[:, k * D + h * 512:k * D + (h + 1) * 512],
                                         start=(k == 0), stop=(k == 7))
            return last
        o("tensor", qkv, reads=[xnT_bl[s2], wq_sb_b, wk_sb_b, wv_sb_b],
          writes=PA_bl + PB_bl + PC_bl)

    def stD(t):
        slot = t % 4
        ss = slot * 8
        sb_ = stat_bl[slot]
        tb = t % NTB
        o("scalar", (lambda tb=tb: lambda s: s.copy(
            Kst[:, tb * D:(tb + 1) * D], PB[:]))(),
          reads=PB_bl, writes=[Kst_bl[tb]])
        v16_, v16_b_ = (v16, v16_b) if t % 2 == 0 else (v16b_t, v16b_bl)
        o("scalar", (lambda v16_=v16_: lambda s: s.copy(v16_[:], PC[:]))(),
          reads=PC_bl, writes=[v16_b_])
        o("scalar", (lambda t=t, v16_=v16_: lambda s: s.dma_start(
            out=sV_h[t * TP:(t + 1) * TP, :], in_=v16_[:]))(),
          reads=[v16_b_], writes=[sV_b[t]], dma=True)
        o("vector", (lambda ss=ss: lambda v: v.reduce_max(
            stat[:, ss + 7:ss + 8], PA[:], axis=AX.X))(),
          reads=PA_bl, writes=[sb_])
        o("scalar", (lambda ss=ss: lambda s: s.mul(
            stat[:, ss + 6:ss + 7], stat[:, ss + 7:ss + 8], -SCALE))(),
          reads=[sb_], writes=[sb_])
        o("scalar", (lambda ss=ss: lambda s: s.activation(
            e_sb[:], PA[:], AF.Exp, bias=stat[:, ss + 6:ss + 7], scale=SCALE,
            accum_out=stat[:, ss + 5:ss + 6]))(),
          reads=PA_bl + [sb_], writes=[e_sb_b, sb_])
        o("vector", (lambda ss=ss: lambda v: v.reciprocal(
            stat[:, ss + 4:ss + 5], stat[:, ss + 5:ss + 6]))(),
          reads=[sb_], writes=[sb_])
        o("vector", (lambda ss=ss: lambda v: v.tensor_scalar(
            rep_s[:], ones_f[:], stat[:, ss + 4:ss + 5], None, ALU.mult))(),
          reads=[ones_b, sb_], writes=[rep_s_b])
        o("vector", lambda v: v.tensor_tensor(w2_s[:], PA[:], e_sb[:], ALU.mult),
          reads=PA_bl + [e_sb_b], writes=[w2_s_b])
        accs_, accs_b_ = (accs, accs_b) if t % 2 == 0 else (accs2_t, accs2_bl)
        xslot = t % NXB
        o("vector", (lambda xslot=xslot, accs_=accs_: lambda v: v.tensor_tensor(
            accs_[:], xbuf[xslot], PA[:], ALU.add))(),
          reads=[xbuf_bl[xslot]] + PA_bl, writes=[accs_b_])
        o("sync", (lambda t=t, accs_=accs_: lambda s: s.dma_start(
            out=sacc_h[t * TP:(t + 1) * TP, 0:512], in_=accs_[:, 0:512]))(),
          reads=[accs_b_], writes=[sacc_b[t]], dma=True)
        o("scalar", (lambda t=t, accs_=accs_: lambda s: s.dma_start(
            out=sacc_h[t * TP:(t + 1) * TP, 512:D], in_=accs_[:, 512:D]))(),
          reads=[accs_b_], writes=[sacc_b[t]], dma=True)
        for h in range(2):
            o("tensor", (lambda h=h: lambda te: te.matmul(
                P7[:], rep_s[:], w2_s[:, h * 512:(h + 1) * 512],
                start=True, stop=True))(),
              reads=[rep_s_b, w2_s_b], writes=[P7_b])
            o("vector", (lambda h=h: lambda v: v.tensor_tensor(
                gqbc[:, h * 512:(h + 1) * 512], gqbc[:, h * 512:(h + 1) * 512],
                P7[:], ALU.add))(),
              reads=[P7_b], writes=[gqbc_b])

    def p2_tile(t):
        slot = t % 4
        ss = slot * 8
        sb_ = stat_bl[slot]
        tb = t % NTB
        o("vector", (lambda tb=tb: lambda v: v.tensor_tensor(
            p2_pbuf[:], gqbc[:], Kst[:, tb * D:(tb + 1) * D], ALU.mult))(),
          reads=[gqbc_b, Kst_bl[tb]], writes=[p2_pbuf_b])
        o("vector", (lambda ss=ss: lambda v: v.reduce_max(
            stat[:, ss + 7:ss + 8], p2_pbuf[:], axis=AX.X))(),
          reads=[p2_pbuf_b], writes=[sb_])
        o("scalar", (lambda ss=ss: lambda s: s.mul(
            stat[:, ss + 6:ss + 7], stat[:, ss + 7:ss + 8], -SCALE))(),
          reads=[sb_], writes=[sb_])
        o("scalar", (lambda ss=ss: lambda s: s.activation(
            p2_e[:], p2_pbuf[:], AF.Exp, bias=stat[:, ss + 6:ss + 7], scale=SCALE,
            accum_out=stat[:, ss + 5:ss + 6]))(),
          reads=[p2_pbuf_b, sb_], writes=[p2_e_b, sb_])
        o("vector", (lambda ss=ss: lambda v: v.reciprocal(
            stat[:, ss + 4:ss + 5], stat[:, ss + 5:ss + 6]))(),
          reads=[sb_], writes=[sb_])
        o("vector", (lambda ss=ss: lambda v: v.tensor_scalar(
            p2_rep[:], ones_f[:], stat[:, ss + 4:ss + 5], None, ALU.mult))(),
          reads=[ones_b, sb_], writes=[p2_rep_b])
        o("vector", lambda v: v.tensor_tensor(p2_w2[:], p2_pbuf[:], p2_e[:], ALU.mult),
          reads=[p2_pbuf_b, p2_e_b], writes=[p2_w2_b])
        for h in range(2):
            o("tensor", (lambda h=h: lambda te: te.matmul(
                P7[:], p2_rep[:], p2_w2[:, h * 512:(h + 1) * 512],
                start=True, stop=True))(),
              reads=[p2_rep_b, p2_w2_b], writes=[P7_b])
            o("vector", (lambda h=h: lambda v: v.tensor_tensor(
                gkbc[:, h * 512:(h + 1) * 512], gkbc[:, h * 512:(h + 1) * 512],
                P7[:], ALU.add))(),
              reads=[P7_b], writes=[gkbc_b])

    def p1_batch(bi, rate, drip=None):
        base = bi * NTB
        stA(base)
        stA(base + 1)
        for s_ in range(NTB + 2):
            if s_ + 2 < NTB:
                stA(base + s_ + 2)
            if 2 <= s_ <= NTB + 1:
                stD(base + s_ - 2)
            if 1 <= s_ <= NTB:
                stC(base + s_ - 1)
            if s_ < NTB:
                stB(base + s_)
            drip_passA(rate)
            if drip is not None:
                drip()

    passB = ([("wd", k) for k in range(32)] + [("wu", k) for k in range(32)]
             + [("wg", k) for k in range(24)] + [("wg", k) for k in range(24, 32)])

    wf32a = wf32 + [wf32x_t[:, i * D:(i + 1) * D] for i in range(3)]
    wf32a_bl = list(wf32_bl) + list(wf32x_bl)

    def emit_passB(idx):
        nm, k = passB[idx]
        slot = idx % 2
        lslot = idx % 2 if idx < 40 else 2 + (idx % 3)
        if nm == "wd":
            src_ = wdT_h[k * 128:(k + 1) * 128, :]
            ridx = 10
            oap, tgt_b = wdst[slot], wdst_bl[slot]
        else:
            w_hh = wgT_h if nm == "wg" else wuT_h
            src_ = w_hh[(k // 4) * 128:(k // 4 + 1) * 128, (k % 4) * D:(k % 4 + 1) * D]
            ridx = 8 if nm == "wg" else 9
            tgt = wgT_sb if nm == "wg" else wuT_sb
            tgt_b = wgT_sb_b if nm == "wg" else wuT_sb_b
            oap = tgt[:, (k // 4) * I + (k % 4) * D:(k // 4) * I + (k % 4 + 1) * D]
        o(["sync", "scalar", "gpsimd"][idx % 3],
          (lambda src_=src_, lslot=lslot:
           lambda s: s.dma_start(out=wf32a[lslot], in_=src_))(),
          writes=[wf32a_bl[lslot]], dma=True)
        o("scalar", (lambda lslot=lslot, ridx=ridx: lambda s: s.activation(
            tstA[:], wf32a[lslot], AF.Sign, scale=wstat[:, ridx:ridx + 1],
            bias=bm5[:, 0:1]))(),
          reads=[wf32a_bl[lslot], wstat_b, bm5_b], writes=[tstA_b])
        o("scalar", (lambda lslot=lslot, ridx=ridx, oap=oap: lambda s: s.activation(
            oap, wf32a[lslot], AF.Sign, scale=wstat[:, ridx:ridx + 1],
            bias=bp5[:, 0:1]))(),
          reads=[wf32a_bl[lslot], wstat_b, bp5_b], writes=[tgt_b])
        o("vector", (lambda oap=oap: lambda v: v.tensor_tensor(
            oap, oap, tstA[:], ALU.add))(),
          reads=[tstA_b, tgt_b], writes=[tgt_b])
        if nm == "wd":
            o("gpsimd" if idx % 2 == 0 else "scalar",
              (lambda k=k, slot=slot: lambda s: s.dma_start(
                  out=swdT_h[k * 128:(k + 1) * 128, :], in_=wdst[slot]))(),
              reads=[tgt_b], writes=[swdT_b[k]], dma=True)

    pb_i = [0]

    def drip_passB(n, limit=None):
        lim = len(passB) if limit is None else limit
        for _ in range(n):
            if pb_i[0] < lim:
                emit_passB(pb_i[0])
                pb_i[0] += 1

    p1_batch(0, 2)
    for t in range(NTB):
        p2_tile(t)
    o("vector", lambda v: v.tensor_copy(gk_s[:, 0:D], gkbc[:]),
      reads=[gkbc_b], writes=[gk_s_b])
    o("vector", lambda v: v.memset(gqbc[:], 0.0), writes=[gqbc_b])
    o("vector", lambda v: v.memset(gkbc[:], 0.0), writes=[gkbc_b])
    p1_batch(1, 4)
    drip_passA(len(passA))

    for idx in range(3):
        o("vector", (lambda idx=idx: lambda v: v.reduce_sum(
            wstat[:, idx:idx + 1], wsum[:, idx * 32:(idx + 1) * 32], axis=AX.X))(),
          reads=[wsum_b], writes=[wstat_b])
    o("tensor", lambda te: te.matmul(P7[:, 0:16], ones_f[:], wstat[:, 0:16],
                                     start=True, stop=True),
      reads=[ones_b, wstat_b], writes=[P7_b])
    o("vector", lambda v: v.tensor_scalar_mul(wstat[:, 4:7], P7[:, 0:3], 1.0 / (I * D)),
      reads=[P7_b], writes=[wstat_b])
    o("vector", lambda v: v.tensor_scalar_max(wstat[:, 4:7], wstat[:, 4:7], 1e-5),
      reads=[wstat_b], writes=[wstat_b])
    o("vector", lambda v: v.reciprocal(wstat[:, 8:11], wstat[:, 4:7]),
      reads=[wstat_b], writes=[wstat_b])
    o("vector", lambda v: v.tensor_scalar_mul(wstat[:, 12:15], wstat[:, 4:7], 0.5),
      reads=[wstat_b], writes=[wstat_b])

    for t in range(NTB, NT):
        p2_tile(t)
        drip_passB(6, 88)
    o("vector", lambda v: v.tensor_copy(gk_s[:, D:2 * D], gkbc[:]),
      reads=[gkbc_b], writes=[gk_s_b])
    drip_passB(len(passB))

    gate_slots = [(PB[:, 0:512], PB_bl[0]), (PB[:, 512:1024], PB_bl[1])]
    up_slots = [(PA[:, 0:512], PA_bl[0]), (PA[:, 512:1024], PA_bl[1]),
                (PC[:, 0:512], PC_bl[0]), (PC[:, 512:1024], PC_bl[1])]

    def f1A(t):
        s2 = t % 2
        o("sync", (lambda t=t, s2=s2: lambda s: s.dma_start(
            out=f_xi[s2], in_=sxi_h[t * TP:(t + 1) * TP, :]))(),
          reads=[sxi_b[t]], writes=[f_xi_bl[s2]], dma=True)

    def f1_head(t):
        s2 = t % 2

        tr_pingpong(f_xi[s2], f_xi_bl[s2], f_xiT[s2], f_xiT_bl[s2])
        o("vector", (lambda t=t: lambda v: v.tensor_scalar(
            cgv[:, t:t + 1], xinvv[:, t:t + 1], wstat[:, 12:13], None, ALU.mult))(),
          reads=[xinvv_b, wstat_b], writes=[cgv_b])
        o("vector", (lambda t=t: lambda v: v.tensor_scalar(
            cuv[:, t:t + 1], xinvv[:, t:t + 1], wstat[:, 13:14], None, ALU.mult))(),
          reads=[xinvv_b, wstat_b], writes=[cuv_b])

    def f1_jj(t, jj):
        s2 = t % 2
        gps, gpb = gate_slots[jj % 2]
        ups, upb = up_slots[jj % 4]

        def mmjj(te, jj=jj, gps=gps, ups=ups, s2=s2):
            last = None
            for k in range(8):
                lhsT = f_xiT[s2][:, k * 128:(k + 1) * 128]
                last = te.matmul(gps, lhsT,
                                 wgT_sb[:, k * I + jj * 512:k * I + (jj + 1) * 512],
                                 start=(k == 0), stop=(k == 7))
                last = te.matmul(ups, lhsT,
                                 wuT_sb[:, k * I + jj * 512:k * I + (jj + 1) * 512],
                                 start=(k == 0), stop=(k == 7))
            return last
        o("tensor", mmjj, reads=[f_xiT_bl[s2], wgT_sb_b, wuT_sb_b],
          writes=[gpb, upb])
        o("scalar", (lambda t=t, jj=jj, gps=gps: lambda s: s.activation(
            f_g[jj % 2], gps, AF.Silu, scale=cgv[:, t:t + 1]))(),
          reads=[gpb, cgv_b], writes=[f_g_bl[jj % 2]])
        o("vector", (lambda t=t, jj=jj, ups=ups: lambda v: v.scalar_tensor_tensor(
            f_h[:, jj * 512:(jj + 1) * 512], ups, cuv[:, t:t + 1],
            f_g[jj % 2], ALU.mult, ALU.mult))(),
          reads=[upb, cuv_b, f_g_bl[jj % 2]], writes=[f_h_b])

    def f1_rowmax(t):
        slot = t % 4
        ss = slot * 8
        sb_ = stat_bl[slot]
        o("vector", (lambda ss=ss: lambda v: v.reduce_max(
            stat[:, ss + 7:ss + 8], f_h[:], axis=AX.X, apply_absolute_value=True))(),
          reads=[f_h_b], writes=[sb_])

    def f1C_head(t):
        slot = t % 4
        ss = slot * 8
        sb_ = stat_bl[slot]
        o("vector", (lambda ss=ss: lambda v: v.tensor_scalar_max(
            stat[:, ss + 7:ss + 8], stat[:, ss + 7:ss + 8], 1e-5))(),
          reads=[sb_], writes=[sb_])
        o("vector", (lambda t=t, ss=ss: lambda v: v.tensor_scalar_mul(
            hinvv[:, t:t + 1], stat[:, ss + 7:ss + 8], 1.0 / 127.0))(),
          reads=[sb_], writes=[hinvv_b])
        o("vector", (lambda t=t: lambda v: v.tensor_scalar(
            cdv[:, t:t + 1], hinvv[:, t:t + 1], wstat[:, 14:15], None, ALU.mult))(),
          reads=[hinvv_b, wstat_b], writes=[cdv_b])
        o("vector", (lambda ss=ss: lambda v: v.reciprocal(
            stat[:, ss + 7:ss + 8], stat[:, ss + 7:ss + 8]))(),
          reads=[sb_], writes=[sb_])
        o("vector", (lambda t=t, ss=ss: lambda v: v.tensor_scalar_mul(
            hsv[:, t:t + 1], stat[:, ss + 7:ss + 8], 127.0))(),
          reads=[sb_], writes=[hsv_b])

    def f1C_pair(t, p, jj_emit):
        qA, qB = 2 * p, 2 * p + 1
        hhA = f_h[:, qA * D:(qA + 1) * D]
        hhB = f_h[:, qB * D:(qB + 1) * D]
        o("vector", (lambda t=t, hhA=hhA: lambda v: v.tensor_scalar(
            fl1[:], hhA, hsv[:, t:t + 1], 128.5, ALU.mult, ALU.add))(),
          reads=[f_h_b, hsv_b], writes=[fl1_b])
        o("vector", (lambda t=t, hhB=hhB: lambda v: v.tensor_scalar(
            fl2[:], hhB, hsv[:, t:t + 1], 128.5, ALU.mult, ALU.add))(),
          reads=[f_h_b, hsv_b], writes=[fl2_b])
        for j in range(8):
            cc = float(2 ** (7 - j))
            scale = (2.0 ** 40) if j == 0 else -(2.0 ** 40)
            op1 = ALU.subtract if j == 0 else ALU.add
            o("scalar", (lambda j=j, scale=scale: lambda s: s.activation(
                rA[:], fl1[:], AF.Relu, scale=scale, bias=cbias[:, j:j + 1]))(),
              reads=[fl1_b, cbias_b], writes=[rA_b])
            o("vector", (lambda cc=cc, op1=op1: lambda v: v.scalar_tensor_tensor(
                fl1[:], rA[:], cc, fl1[:], ALU.min, op1))(),
              reads=[rA_b], writes=[fl1_b])
            o("scalar", (lambda j=j, scale=scale: lambda s: s.activation(
                rB[:], fl2[:], AF.Relu, scale=scale, bias=cbias[:, j:j + 1]))(),
              reads=[fl2_b, cbias_b], writes=[rB_b])
            o("vector", (lambda cc=cc, op1=op1: lambda v: v.scalar_tensor_tensor(
                fl2[:], rB[:], cc, fl2[:], ALU.min, op1))(),
              reads=[rB_b], writes=[fl2_b])
        o("vector", (lambda t=t, qA=qA, hhA=hhA: lambda v: v.scalar_tensor_tensor(
            f_hq[:, qA * D:(qA + 1) * D], hhA, hsv[:, t:t + 1], fl1[:],
            ALU.mult, ALU.add))(),
          reads=[f_h_b, hsv_b, fl1_b], writes=[f_hq_b])
        o("vector", (lambda t=t, qB=qB, hhB=hhB: lambda v: v.scalar_tensor_tensor(
            f_hq[:, qB * D:(qB + 1) * D], hhB, hsv[:, t:t + 1], fl2[:],
            ALU.mult, ALU.add))(),
          reads=[f_h_b, hsv_b, fl2_b], writes=[f_hq_b])
        jj_emit()

    def f1C_dma(t):
        o("gpsimd", (lambda t=t: lambda s: s.dma_start(
            out=shqT_h[t * TP:(t + 1) * TP, 0:I // 2], in_=f_hq[:, 0:I // 2]))(),
          reads=[f_hq_b], writes=[shqT_b[t]], dma=True)
        o("gpsimd", (lambda t=t: lambda s: s.dma_start(
            out=shqT_h[t * TP:(t + 1) * TP, I // 2:I], in_=f_hq[:, I // 2:I]))(),
          reads=[f_hq_b], writes=[shqT_b[t]], dma=True)

    f1A(0)
    f1A(1)
    f1_head(0)
    for jj in range(8):
        f1_jj(0, jj)
    f1_rowmax(0)
    for s_ in range(1, NT + 1):
        if s_ + 1 < NT:
            f1A(s_ + 1)
        if s_ < NT:
            f1_head(s_)
        f1C_head(s_ - 1)
        for p_ in range(2):
            def jj_emit(s_=s_, p_=p_):
                if s_ < NT:
                    for j4 in range(4):
                        f1_jj(s_, 4 * p_ + j4)
            f1C_pair(s_ - 1, p_, jj_emit)
        f1C_dma(s_ - 1)
        if s_ < NT:
            f1_rowmax(s_)

    for k in range(8):
        o("gpsimd", (lambda k=k: lambda g: g.dma_start(
            out=wr_sb[:, k * D:(k + 1) * D], in_=wr_h[k * 128:(k + 1) * 128, :]))(),
          writes=[wr_sb_b], dma=True)
    for k in range(32):
        o("sync", (lambda k=k: lambda s: s.dma_start(
            out=wdT_sb[:, k * D:(k + 1) * D], in_=swdT_h[k * 128:(k + 1) * 128, :]))(),
          reads=[swdT_b[k]], writes=[wdT_sb_b], dma=True)
    for h2 in range(2):
        def rsmm(te, h2=h2):
            last = None
            for k in range(32):
                last = te.matmul(P7[0:1, :], halfv[:],
                                 wdT_sb[:, k * D + h2 * 512:k * D + (h2 + 1) * 512],
                                 start=(k == 0), stop=(k == 31))
            return last
        o("tensor", rsmm, reads=[halfv_b, wdT_sb_b], writes=[P7_b])
        o("vector", (lambda h2=h2: lambda v: v.tensor_copy(
            nrs_sb[:, h2 * 512:(h2 + 1) * 512], P7[0:1, :]))(),
          reads=[P7_b], writes=[nrs_sb_b])
    for h2 in range(2):
        o("tensor", (lambda h2=h2: lambda te: te.matmul(
            P7[:], ones_f[0:1, :], nrs_sb[:, h2 * 512:(h2 + 1) * 512],
            start=True, stop=True))(),
          reads=[ones_b, nrs_sb_b], writes=[P7_b])
        o("vector", (lambda h2=h2: lambda v: v.tensor_copy(
            corr_bc[:, h2 * 512:(h2 + 1) * 512], P7[:]))(),
          reads=[P7_b], writes=[corr_bc_b])

    def f2A(t):
        s2 = t % 2
        o("sync", (lambda t=t, s2=s2: lambda s: s.dma_start(
            out=z_hq[s2], in_=shqT_h[t * TP:(t + 1) * TP, :]))(),
          reads=[shqT_b[t]], writes=[z_hq_bl[s2]], dma=True)
        o("scalar", (lambda t=t, s2=s2: lambda s: s.dma_start(
            out=z_v[s2], in_=sV_h[t * TP:(t + 1) * TP, :]))(),
          reads=[sV_b[t]], writes=[z_v_bl[s2]], dma=True)
        o("scalar", (lambda t=t: lambda s: s.dma_start(
            out=z_acc[t % 4], in_=sacc_h[t * TP:(t + 1) * TP, :]))(),
          reads=[sacc_b[t]], writes=[z_acc_bl[t % 4]], dma=True)

    def f2B(t):
        s2 = t % 2
        bi = t // NTB
        o("vector", (lambda bi=bi, s2=s2: lambda v: v.tensor_tensor(
            z_u[:], gk_s[:, bi * D:(bi + 1) * D], z_v[s2], ALU.mult))(),
          reads=[gk_s_b, z_v_bl[s2]], writes=[z_u_b])

        tr_pingpong(z_u, z_u_b, z_uT[s2], z_uT_bl[s2])
        for r in range(4):
            tr_pingpong(z_hq[s2], z_hq_bl[s2],
                        z_hqT[s2][:, r * 1024:(r + 1) * 1024], z_hqT_bl[s2],
                        src_off=r * 1024)

    def f2C(t):
        s2 = t % 2

        def rmm(te, s2=s2):
            last = None
            for k in range(8):
                lhsT = z_uT[s2][:, k * 128:(k + 1) * 128]
                for h in range(2):
                    last = te.matmul(PB[:, h * 512:(h + 1) * 512], lhsT,
                                     wr_sb[:, k * D + h * 512:k * D + (h + 1) * 512],
                                     start=(k == 0), stop=(k == 7))
            return last
        o("tensor", rmm, reads=[z_uT_bl[s2], wr_sb_b], writes=PB_bl)

        dps = PA if (t % 2 == 0) else PC
        dpb = PA_bl if (t % 2 == 0) else PC_bl

        def dmm(te, dps=dps, s2=s2):
            last = None
            for k in range(32):
                lhsT = z_hqT[s2][:, k * 128:(k + 1) * 128]
                for h in range(2):
                    last = te.matmul(dps[:, h * 512:(h + 1) * 512], lhsT,
                                     wdT_sb[:, k * D + h * 512:k * D + (h + 1) * 512],
                                     start=(k == 0), stop=(k == 31))
            return last
        o("tensor", dmm, reads=[z_hqT_bl[s2], wdT_sb_b], writes=dpb)

    def f2D(t):
        s2 = t % 2
        dps = PA if (t % 2 == 0) else PC
        dpb = PA_bl if (t % 2 == 0) else PC_bl
        o("vector", (lambda t=t: lambda v: v.tensor_tensor(
            z_a2[:], z_acc[t % 4], PB[:], ALU.add))(),
          reads=[z_acc_bl[t % 4]] + PB_bl, writes=[z_a2_b])
        o("vector", (lambda t=t: lambda v: v.scalar_tensor_tensor(
            z_a2[:], corr_bc[:], cdv[:, t:t + 1], z_a2[:], ALU.mult, ALU.add))(),
          reads=[corr_bc_b, cdv_b, z_a2_b], writes=[z_a2_b])
        for h in range(2):
            o("vector", (lambda t=t, h=h, dps=dps, s2=s2: lambda v: v.scalar_tensor_tensor(
                z_y[s2][:, h * 512:(h + 1) * 512], dps[:, h * 512:(h + 1) * 512],
                cdv[:, t:t + 1], z_a2[:, h * 512:(h + 1) * 512], ALU.mult, ALU.add))(),
              reads=[dpb[h], cdv_b, z_a2_b], writes=[z_y_bl[s2]])
        o("sync", (lambda t=t, s2=s2: lambda s: s.dma_start(
            out=y_h[t * TP:(t + 1) * TP, :], in_=z_y[s2]))(),
          reads=[z_y_bl[s2]], writes=[], dma=True)

    f2A(0)
    f2A(1)
    for s_ in range(NT + 2):
        if s_ < NT:
            f2B(s_)
        if s_ >= 2:
            f2D(s_ - 2)
        if 1 <= s_ <= NT:
            f2C(s_ - 1)
        if s_ + 2 < NT:
            f2A(s_ + 2)

    from contextlib import ExitStack
    with ExitStack() as st:
        semh = {s: st.enter_context(nc.semaphore("sem_" + s)) for s in ALL_SEMS}
        block = st.enter_context(nc.Block())

        def mk(engname):
            def _f(eng):
                for waits, build, sem, amt in pr.items[engname]:
                    for s_, v_ in sorted(waits.items()):
                        eng.wait_ge(semh[s_], v_)
                    inst = build(eng)
                    inst.then_inc(semh[sem], amt)
                if engname == "sync":
                    for ln in DMA_LANES["DS"]:
                        if pr.counts[ln]:
                            eng.wait_ge(semh[ln], pr.counts[ln])
            return _f

        block.sync(mk("sync"))
        block.gpsimd(mk("gpsimd"))
        block.vector(mk("vector"))
        block.scalar(mk("scalar"))
        block.tensor(mk("tensor"))

    return nc


_NC_CACHE = []


def kernel(L0_attn_x, L0_attn_Wq, L0_attn_Wk, L0_attn_Wv, L0_attn_Wr,
           L0_ffn_w_gate, L0_ffn_w_up, L0_ffn_w_down):
    x = np.ascontiguousarray(np.asarray(L0_attn_x, dtype=np.float32))
    wq = np.ascontiguousarray(np.asarray(L0_attn_Wq, dtype=np.float32))
    wk = np.ascontiguousarray(np.asarray(L0_attn_Wk, dtype=np.float32))
    wv = np.ascontiguousarray(np.asarray(L0_attn_Wv, dtype=np.float32))
    wr = np.ascontiguousarray(np.asarray(L0_attn_Wr, dtype=np.float32))
    wgT = np.ascontiguousarray(np.asarray(L0_ffn_w_gate, dtype=np.float32).T)
    wuT = np.ascontiguousarray(np.asarray(L0_ffn_w_up, dtype=np.float32).T)
    wdT = np.ascontiguousarray(np.asarray(L0_ffn_w_down, dtype=np.float32).T)
    if not _NC_CACHE:
        _NC_CACHE.append(build_nc())
    nc = _NC_CACHE[0]
    in_maps = []
    for c in range(8):
        in_maps.append({
            "x": np.ascontiguousarray(x[2 * c:2 * c + 2].reshape(NT * TP, D)),
            "wq": wq, "wk": wk, "wv": wv, "wr": wr,
            "wgT": wgT, "wuT": wuT, "wdT": wdT,
        })
    res = run_bass_kernel_spmd(nc, in_maps, core_ids=list(range(8)))
    out = np.concatenate([r["y"].reshape(B, S, D) for r in res.results], axis=0)
    return out
